# version 03 (final)
# baseline (speedup 1.0000x reference)
import numpy as np
from contextlib import ExitStack

import concourse.bass as bass
import concourse.mybir as mybir
from concourse.bass_utils import run_bass_kernel_spmd

F32 = mybir.dt.float32
I32 = mybir.dt.int32
BF16 = mybir.dt.bfloat16
AF = mybir.ActivationFunctionType
ALU = mybir.AluOpType
AX = mybir.AxisListType

D = 2048
TLOC = 1024
NCH = D // 128
T_ALL = 8192
SEQ = 4096
NE = 12
FEXP = 256
CAP = 854
NEG = -60000.0
LN_EPS = 1e-5
RMS_EPS = 1.1920928955078125e-07
LAMBDA_INIT = 0.8 - 0.6 * float(np.exp(-0.3 * 12))

TWO_PI = float(np.float64(2 * np.pi))
C1 = float(np.float32(TWO_PI))
C2 = float(np.float32(np.float64(TWO_PI) - np.float64(np.float32(TWO_PI))))
C3 = float(np.float32(np.float64(TWO_PI) - np.float64(np.float32(TWO_PI)) - np.float64(C2)))
INV_2PI = float(np.float32(1.0 / TWO_PI))
PI = float(np.float32(np.pi))
HALF_PI = float(np.float32(np.pi / 2))
LN1E4 = float(np.log(10000.0))


class Ctx:

    def __init__(self, name):
        self.nc = bass.Bass("TRN2", target_bir_lowering=False)
        self.es = ExitStack()
        self.blk = None
        self.sem = {}
        for s in ("sl", "gl", "cv", "cs", "ct", "cg", "so"):
            self.sem[s] = self.es.enter_context(self.nc.semaphore(s))
        self.cnt = {s: 0 for s in self.sem}

    def inc(self, s, instr, n=1):
        instr.then_inc(self.sem[s], n)
        self.cnt[s] += n
        return instr

    def dma(self, s, instr):
        return self.inc(s, instr, 16)

    def wait(self, eng, *names):
        for s in names:
            if self.cnt[s]:
                eng.wait_ge(self.sem[s], self.cnt[s])

    def chain(self, eng, s, instr):
        self.inc(s, instr)
        eng.wait_ge(self.sem[s], self.cnt[s])
        return instr

    def sb(self, name, shape, dt=F32):
        return self.nc.alloc_sbuf_tensor(name, shape, dt)

    def ps(self, name, shape, dt=F32):
        return self.nc.alloc_psum_tensor(name, shape, dt)


def emit_sin(ctx, theta, out_sb, scr, scr2, scr_i, shift, n):
    blk = ctx.blk

    @blk.vector
    def _(v):
        ctx.wait(v, "ct", "cs")
        ctx.chain(v, "cv", v.tensor_scalar(out=scr[:, 0:n], in0=theta[:, 0:n],
                                           scalar1=float(shift), scalar2=None, op0=ALU.add))
        ctx.chain(v, "cv", v.tensor_scalar(out=scr2[:, 0:n], in0=scr[:, 0:n],
                                           scalar1=INV_2PI, scalar2=None, op0=ALU.mult))
        ctx.chain(v, "cv", v.tensor_copy(out=scr_i[:, 0:n], in_=scr2[:, 0:n]))
        ctx.chain(v, "cv", v.tensor_copy(out=scr2[:, 0:n], in_=scr_i[:, 0:n]))
        for cc_ in (C1, C2, C3):
            ctx.chain(v, "cv", v.scalar_tensor_tensor(out=scr[:, 0:n], in0=scr2[:, 0:n],
                                                      scalar=-cc_, in1=scr[:, 0:n],
                                                      op0=ALU.mult, op1=ALU.add))
        ctx.chain(v, "cv", v.tensor_scalar(out=scr2[:, 0:n], in0=scr[:, 0:n], scalar1=-PI,
                                           scalar2=C1, op0=ALU.is_lt, op1=ALU.mult))
        ctx.chain(v, "cv", v.tensor_add(scr[:, 0:n], scr[:, 0:n], scr2[:, 0:n]))
        ctx.chain(v, "cv", v.tensor_scalar(out=scr2[:, 0:n], in0=scr[:, 0:n], scalar1=PI,
                                           scalar2=C1, op0=ALU.is_gt, op1=ALU.mult))
        ctx.chain(v, "cv", v.tensor_sub(scr[:, 0:n], scr[:, 0:n], scr2[:, 0:n]))

    @blk.scalar
    def _(s):
        ctx.wait(s, "cv")
        ctx.chain(s, "cs", s.activation(out=out_sb[:, 0:n], in_=scr[:, 0:n], func=AF.Sin))


def emit_ln(ctx, x_view, xln_view, w, n_tok=TLOC):
    blk = ctx.blk
    nh = n_tok // 512

    @blk.scalar
    def _(s):
        ctx.wait(s, "cv", "gl", "sl", "ct")
        for ci in range(NCH):
            ctx.inc("cs", s.activation(out=w["sq"][:, ci * n_tok:(ci + 1) * n_tok],
                                       in_=x_view(ci), func=AF.Square))
        s.wait_ge(ctx.sem["cs"], ctx.cnt["cs"])

    @blk.tensor
    def _(t):
        ctx.wait(t, "cs")
        for ci in range(NCH):
            for nn in range(nh):
                ctx.inc("ct", t.matmul(w["pC"][0:1, nn * 512:(nn + 1) * 512], w["ones_bf"][:, 0:1],
                                       x_view(ci)[:, nn * 512:(nn + 1) * 512],
                                       start=(ci == 0), stop=(ci == NCH - 1)))
                ctx.inc("ct", t.matmul(w["pD"][0:1, nn * 512:(nn + 1) * 512], w["ones_bf"][:, 0:1],
                                       w["sq"][:, ci * n_tok + nn * 512:ci * n_tok + (nn + 1) * 512],
                                       start=(ci == 0), stop=(ci == NCH - 1)))

    @blk.scalar
    def _(s):
        ctx.wait(s, "ct")
        ctx.chain(s, "cs", s.mul(w["row_mu"][:, 0:n_tok], w["pC"][0:1, 0:n_tok], 1.0 / D))
        ctx.chain(s, "cs", s.mul(w["row_e2"][:, 0:n_tok], w["pD"][0:1, 0:n_tok], 1.0 / D))

    @blk.vector
    def _(v):
        ctx.wait(v, "cs")
        ctx.chain(v, "cv", v.tensor_mul(w["row_r"][:, 0:n_tok], w["row_mu"][:, 0:n_tok],
                                        w["row_mu"][:, 0:n_tok]))
        ctx.chain(v, "cv", v.tensor_sub(w["row_e2"][:, 0:n_tok], w["row_e2"][:, 0:n_tok],
                                        w["row_r"][:, 0:n_tok]))

    @blk.scalar
    def _(s):
        ctx.wait(s, "cv")
        ctx.chain(s, "cs", s.activation(out=w["row_r"][:, 0:n_tok], in_=w["row_e2"][:, 0:n_tok],
                                        func=AF.Sqrt, bias=w["eps1"][0:1, 0:1]))

    @blk.vector
    def _(v):
        ctx.wait(v, "cs")
        ctx.chain(v, "cv", v.reciprocal(w["row_r"][:, 0:n_tok], w["row_r"][:, 0:n_tok]))

    @blk.tensor
    def _(t):
        ctx.wait(t, "cv")
        for nn in range(nh):
            ctx.inc("ct", t.matmul(w["pC"][:, nn * 512:(nn + 1) * 512], w["ones_row"][0:1, :],
                                   w["row_mu"][0:1, nn * 512:(nn + 1) * 512], start=True, stop=True))
            ctx.inc("ct", t.matmul(w["pD"][:, nn * 512:(nn + 1) * 512], w["ones_row"][0:1, :],
                                   w["row_r"][0:1, nn * 512:(nn + 1) * 512], start=True, stop=True))

    @blk.scalar
    def _(s):
        ctx.wait(s, "ct")
        ctx.chain(s, "cs", s.copy(w["mu_b"][:, 0:n_tok], w["pC"][:, 0:n_tok]))
        ctx.chain(s, "cs", s.copy(w["r_b"][:, 0:n_tok], w["pD"][:, 0:n_tok]))

    @blk.vector
    def _(v):
        ctx.wait(v, "cs")
        for ci in range(NCH):
            ctx.chain(v, "cv", v.tensor_sub(w["scr"][:, 0:n_tok], x_view(ci), w["mu_b"][:, 0:n_tok]))
            ctx.chain(v, "cv", v.tensor_mul(xln_view(ci), w["scr"][:, 0:n_tok], w["r_b"][:, 0:n_tok]))


def emit_rope_L0(ctx, src_f32, dst_bf, cc_t, ss_t, pT2, scrT1):
    blk = ctx.blk

    @blk.vector
    def _(v):
        ctx.wait(v, "cs", "ct")
        ctx.inc("cv", v.tensor_mul(scrT1[:, :], src_f32[:, :], cc_t[:, :]))
        ctx.chain(v, "cv", v.tensor_mul(pT2[:, :], src_f32[:, :], ss_t[:, :]))
        for vv in range(2):
            b = 64 * vv
            ctx.chain(v, "cv", v.tensor_sub(dst_bf[b:b + 32, :], scrT1[b:b + 32, :],
                                            pT2[b + 32:b + 64, :]))
            ctx.chain(v, "cv", v.tensor_add(dst_bf[b + 32:b + 64, :], pT2[b:b + 32, :],
                                            scrT1[b + 32:b + 64, :]))


def emit_rope_L1(ctx, src_f32, dst_bf, cc_t, ss_t, pT2, scrT1):
    blk = ctx.blk

    @blk.vector
    def _(v):
        ctx.wait(v, "cs", "ct")
        ctx.inc("cv", v.tensor_mul(scrT1[:, :], src_f32[:, :], cc_t[:, :]))
        ctx.chain(v, "cv", v.tensor_mul(pT2[:, :], src_f32[:, :], ss_t[:, :]))
        ctx.chain(v, "cv", v.tensor_sub(dst_bf[0:64, :], scrT1[0:64, :], pT2[64:128, :]))
        ctx.chain(v, "cv", v.tensor_add(dst_bf[64:128, :], pT2[0:64, :], scrT1[64:128, :]))



def build_launch_A(debug=False):
    ctx = Ctx("A")
    nc = ctx.nc

    x_own = nc.dram_tensor("x_own", [D, TLOC], F32, kind="ExternalInput").ap()
    x_full = nc.dram_tensor("x_full", [8, D, TLOC], F32, kind="ExternalInput").ap()
    gate_wT = nc.dram_tensor("gate_wT", [D, NE], F32, kind="ExternalInput").ap()
    w1T = nc.dram_tensor("w1T", [NE, D, FEXP], F32, kind="ExternalInput").ap()
    w3T = nc.dram_tensor("w3T", [NE, D, FEXP], F32, kind="ExternalInput").ap()
    w2T = nc.dram_tensor("w2T", [NE, FEXP, D], F32, kind="ExternalInput").ap()
    wqT = nc.dram_tensor("wqT", [D, D], F32, kind="ExternalInput").ap()
    wkT = nc.dram_tensor("wkT", [D, D], F32, kind="ExternalInput").ap()
    wvT = nc.dram_tensor("wvT", [D, D], F32, kind="ExternalInput").ap()
    pos_in = nc.dram_tensor("pos", [1, TLOC], F32, kind="ExternalInput").ap()
    onehot = nc.dram_tensor("onehot", [1, 8], F32, kind="ExternalInput").ap()
    ident_in = nc.dram_tensor("ident_in", [128, 128], F32, kind="ExternalInput").ap()

    x1_out = nc.dram_tensor("x1_out", [D, TLOC], F32, kind="ExternalOutput").ap()
    q_out = nc.dram_tensor("q_out", [D, TLOC], BF16, kind="ExternalOutput").ap()
    k_out = nc.dram_tensor("k_out", [D, TLOC], BF16, kind="ExternalOutput").ap()
    v_out = nc.dram_tensor("v_out", [D, TLOC], BF16, kind="ExternalOutput").ap()

    w_rows_d = nc.dram_tensor("w_rows_d", [NE, TLOC], F32, kind="Internal")
    if debug:
        dbg_w = nc.dram_tensor("dbg_w", [NE, TLOC], F32, kind="ExternalOutput").ap()
        dbg_rw = nc.dram_tensor("dbg_rw", [1, T_ALL], F32, kind="ExternalOutput").ap()
        dbg_m0 = nc.dram_tensor("dbg_m0", [NE, T_ALL], F32, kind="ExternalOutput").ap()
        dbg_m1 = nc.dram_tensor("dbg_m1", [NE, T_ALL], F32, kind="ExternalOutput").ap()
        dbg_xln = nc.dram_tensor("dbg_xln", [D, TLOC], BF16, kind="ExternalOutput").ap()
    moe_d = nc.dram_tensor("moe_d", [D, TLOC], F32, kind="Internal")

    sb, ps = ctx.sb, ctx.ps
    ones_bf = sb("ones_bf", [128, 1], BF16)
    ones_row = sb("ones_row", [1, 128])
    ident = sb("ident", [128, 128])
    sid_row = sb("sid_row", [1, 8])
    pos_row = sb("pos_row", [1, TLOC])
    gate_sb = sb("gate_sb", [128, NCH * NE], BF16)
    cc_t = sb("cc_t", [128, TLOC])
    ss_t = sb("ss_t", [128, TLOC])
    om_row = sb("om_row", [1, 128])
    iot = sb("iot", [1, 32])
    row_mu = sb("row_mu", [1, TLOC])
    row_e2 = sb("row_e2", [1, TLOC])
    row_r = sb("row_r", [1, TLOC])
    mu_b = sb("mu_b", [128, TLOC])
    r_b = sb("r_b", [128, TLOC])
    scr = sb("scr", [128, TLOC])
    scr2 = sb("scr2", [128, TLOC])
    lg_tm = sb("lg_tm", [128, NE])
    m_tm = sb("m_tm", [128, NE])
    mt_tm = sb("mt_tm", [128, NE])
    l1_c = sb("l1_c", [128, 1])
    l2_c = sb("l2_c", [128, 1])
    rw_c = sb("rw_c", [128, 1])
    tot0 = sb("tot0", [NE, 1])
    car0 = sb("car0", [NE, 1])
    car1 = sb("car1", [NE, 1])
    sid_b = sb("sid_b", [NE, 1])
    w_own = sb("w_own", [NE, TLOC])
    em_scr = sb("em_scr", [NE, 512])
    em_scr2 = sb("em_scr2", [NE, 512])
    rwch = sb("rwch", [1, 512])
    eps1 = sb("eps1", [1, 1])
    ones_rb = sb("ones_rb", [1, 128], BF16)

    xln = sb("xln", [128, NCH * TLOC], BF16)

    def guard(name, shape, dt=F32):
        return nc.sbuf_tensor(name, shape, dt)

    pA = ps("pA", [128, 512])
    pB = ps("pB", [128, 512])
    pC = ps("pC", [128, TLOC])
    pD = ps("pD", [128, TLOC])

    with nc.Block() as blk:
        ctx.blk = blk

        @blk.sync
        def _(sy):
            ctx.dma("sl", sy.dma_start(out=ident[:], in_=ident_in))
            ctx.dma("sl", sy.dma_start(out=pos_row[:], in_=pos_in))
            ctx.dma("sl", sy.dma_start(out=sid_row[:], in_=onehot))

        @blk.gpsimd
        def _(g):
            ctx.dma("gl", g.dma_start(out=gate_sb.ap().rearrange("p (c e) -> p c e", e=NE),
                                      in_=gate_wT.rearrange("(c p) e -> p c e", p=128)))
            ctx.inc("cg", g.iota(iot[:, :], [[1, 32]], base=0, channel_multiplier=0,
                                 allow_small_or_imprecise_dtypes=True))

        @blk.vector
        def _(v):
            ctx.inc("cv", v.memset(ones_bf[:], 1.0))
            ctx.chain(v, "cv", v.memset(ones_row[:], 1.0))
            ctx.chain(v, "cv", v.memset(eps1[:, :], LN_EPS))
            ctx.chain(v, "cv", v.memset(ones_rb[:, :], 1.0))

        @blk.scalar
        def _(s):
            ctx.wait(s, "cg")
            ctx.chain(s, "cs", s.activation(out=iot[:, :], in_=iot[:, :], func=AF.Exp,
                                            scale=-LN1E4 / 31.0))
            for rep in range(4):
                ctx.chain(s, "cs", s.copy(om_row[0:1, rep * 32:(rep + 1) * 32], iot[0:1, :]))

        @blk.tensor
        def _(t):
            ctx.wait(t, "cs", "sl")
            for nn in range(2):
                ctx.inc("ct", t.matmul(pC[:, nn * 512:(nn + 1) * 512], om_row[0:1, :],
                                       pos_row[0:1, nn * 512:(nn + 1) * 512],
                                       start=True, stop=True))

        es_p1 = ExitStack()
        scr_i = es_p1.enter_context(guard("scr_i", [128, TLOC], I32))
        emit_sin(ctx, pC, ss_t, scr, scr2, scr_i, 0.0, TLOC)
        emit_sin(ctx, pC, cc_t, scr, scr2, scr_i, HALF_PI, TLOC)
        es_p1.close()

        es_p2 = ExitStack()
        m0_em = es_p2.enter_context(guard("m0_em", [NE, T_ALL], BF16))
        m1_em = es_p2.enter_context(guard("m1_em", [NE, T_ALL], BF16))
        rw1_row = es_p2.enter_context(guard("rw1_row", [1, T_ALL], BF16))
        rk_blk = es_p2.enter_context(guard("rk_blk", [NE, 2048]))
        sl_in = es_p2.enter_context(guard("sl_in", [128, NCH * 512], BF16))
        sqb = es_p2.enter_context(guard("sqb", [128, NCH * 512], BF16))
        w = dict(pC=pC, pD=pD, ones_bf=ones_bf, ones_row=ones_row, sq=sqb,
                 row_mu=row_mu, row_e2=row_e2, row_r=row_r, mu_b=mu_b, r_b=r_b, scr=scr,
                 eps1=eps1)

        for hs in range(16):
            g_ = hs // 2
            half = hs % 2

            @blk.gpsimd
            def _(g, g_=g_, half=half):
                ctx.wait(g, "ct", "cv", "cs")
                ctx.dma("gl", g.dma_start(
                    out=sl_in.ap().rearrange("p (c t) -> p c t", t=512),
                    in_=x_full[g_].rearrange("(c p) t -> p c t", p=128)[:, :, half * 512:(half + 1) * 512]))

            emit_ln(ctx, lambda ci: sl_in[:, ci * 512:(ci + 1) * 512],
                    lambda ci: sqb[:, ci * 512:(ci + 1) * 512], w, n_tok=512)

            for tau in range(4):
                off = hs * 512 + tau * 128

                @blk.tensor
                def _(t, tau=tau):
                    ctx.wait(t, "cv", "cs")
                    for ci in range(NCH):
                        ctx.inc("ct", t.matmul(
                            pA[:, 0:NE],
                            sqb[:, ci * 512 + tau * 128: ci * 512 + (tau + 1) * 128],
                            gate_sb[:, ci * NE:(ci + 1) * NE],
                            start=(ci == 0), stop=(ci == NCH - 1)))

                @blk.scalar
                def _(s):
                    ctx.wait(s, "ct")
                    ctx.chain(s, "cs", s.copy(lg_tm[:, :], pA[:, 0:NE]))

                @blk.vector
                def _(v):
                    ctx.wait(v, "cs")
                    ctx.chain(v, "cv", v.reduce_max(l1_c[:, :], lg_tm[:, :], axis=AX.X))
                    ctx.chain(v, "cv", v.tensor_scalar(out=m_tm[:, :], in0=lg_tm[:, :],
                                                       scalar1=l1_c[:, 0:1], scalar2=None,
                                                       op0=ALU.is_equal))
                    ctx.chain(v, "cv", v.scalar_tensor_tensor(out=mt_tm[:, :], in0=m_tm[:, :],
                                                              scalar=NEG, in1=lg_tm[:, :],
                                                              op0=ALU.mult, op1=ALU.add))
                    ctx.chain(v, "cv", v.reduce_max(l2_c[:, :], mt_tm[:, :], axis=AX.X))
                    ctx.chain(v, "cv", v.tensor_scalar(out=mt_tm[:, :], in0=lg_tm[:, :],
                                                       scalar1=l2_c[:, 0:1], scalar2=None,
                                                       op0=ALU.is_equal))
                    ctx.chain(v, "cv", v.tensor_sub(rw_c[:, :], l1_c[:, :], l2_c[:, :]))

                @blk.scalar
                def _(s):
                    ctx.wait(s, "cv")
                    ctx.chain(s, "cs", s.activation(out=rw_c[:, :], in_=rw_c[:, :], func=AF.Sigmoid))

                @blk.tensor
                def _(t):
                    ctx.wait(t, "cs")
                    ctx.inc("ct", t.transpose(pB[0:NE, 0:128], m_tm[:, 0:NE], ident[:]))
                    ctx.chain(t, "ct", t.matmul(pA[0:1, 0:128], rw_c[:, 0:1], ident[:],
                                                start=True, stop=True))

                @blk.scalar
                def _(s, off=off):
                    ctx.wait(s, "ct")
                    ctx.chain(s, "cs", s.copy(m0_em[:, off:off + 128], pB[0:NE, 0:128]))
                    ctx.chain(s, "cs", s.copy(rw1_row[0:1, off:off + 128], pA[0:1, 0:128]))

                @blk.tensor
                def _(t):
                    ctx.wait(t, "cs")
                    ctx.inc("ct", t.transpose(pB[0:NE, 0:128], mt_tm[:, 0:NE], ident[:]))

                @blk.scalar
                def _(s, off=off):
                    ctx.wait(s, "ct")
                    ctx.chain(s, "cs", s.copy(m1_em[:, off:off + 128], pB[0:NE, 0:128]))

        @blk.vector
        def _(v):
            ctx.wait(v, "cs")
            ctx.chain(v, "cv", v.memset(car0[:, :], 0.0))
            ctx.chain(v, "cv", v.memset(car1[:, :], 0.0))
            for b4 in range(4):
                o = b4 * 2048
                ctx.chain(v, "cv", v.tensor_tensor_scan(out=rk_blk[:, :], data0=m0_em[:, o:o + 2048],
                                                        data1=m0_em[:, o:o + 2048],
                                                        initial=car0[:, 0:1],
                                                        op0=ALU.add, op1=ALU.bypass))
                ctx.chain(v, "cv", v.tensor_copy(out=car0[:, 0:1], in_=rk_blk[:, 2047:2048]))
                ctx.chain(v, "cv", v.tensor_sub(rk_blk[:, :], rk_blk[:, :], m0_em[:, o:o + 2048]))
                ctx.chain(v, "cv", v.tensor_scalar(out=rk_blk[:, :], in0=rk_blk[:, :],
                                                   scalar1=float(CAP) - 0.5, scalar2=None,
                                                   op0=ALU.is_le))
                ctx.chain(v, "cv", v.tensor_mul(m0_em[:, o:o + 2048], m0_em[:, o:o + 2048],
                                                rk_blk[:, :]))
            ctx.chain(v, "cv", v.tensor_copy(out=tot0[:, 0:1], in_=car0[:, 0:1]))
            for b4 in range(4):
                o = b4 * 2048
                ctx.chain(v, "cv", v.tensor_tensor_scan(out=rk_blk[:, :], data0=m1_em[:, o:o + 2048],
                                                        data1=m1_em[:, o:o + 2048],
                                                        initial=car1[:, 0:1],
                                                        op0=ALU.add, op1=ALU.bypass))
                ctx.chain(v, "cv", v.tensor_copy(out=car1[:, 0:1], in_=rk_blk[:, 2047:2048]))
                ctx.chain(v, "cv", v.tensor_sub(rk_blk[:, :], rk_blk[:, :], m1_em[:, o:o + 2048]))
                ctx.chain(v, "cv", v.tensor_scalar(out=rk_blk[:, :], in0=rk_blk[:, :],
                                                   scalar1=tot0[:, 0:1], scalar2=float(CAP) - 0.5,
                                                   op0=ALU.add, op1=ALU.is_le))
                ctx.chain(v, "cv", v.tensor_mul(m1_em[:, o:o + 2048], m1_em[:, o:o + 2048],
                                                rk_blk[:, :]))

        @blk.vector
        def _(v):
            ctx.chain(v, "cv", v.memset(w_own[:, :], 0.0))

        for g_ in range(8):
            @blk.tensor
            def _(t, g_=g_):
                ctx.wait(t, "cv", "cs")
                ctx.inc("ct", t.matmul(pA[0:NE, 0:1], ones_row[0:1, 0:NE],
                                       sid_row[0:1, g_:g_ + 1], start=True, stop=True))

            @blk.scalar
            def _(s):
                ctx.wait(s, "ct")
                ctx.chain(s, "cs", s.copy(sid_b[:, 0:1], pA[0:NE, 0:1]))

            for nn in range(2):
                o = g_ * TLOC + nn * 512

                @blk.vector
                def _(v, o=o, nn=nn):
                    ctx.wait(v, "cs")
                    ctx.chain(v, "cv", v.tensor_scalar(out=rwch[0:1, :], in0=rw1_row[0:1, o:o + 512],
                                                       scalar1=-1.0, scalar2=1.0,
                                                       op0=ALU.mult, op1=ALU.add))

                @blk.tensor
                def _(t, o=o):
                    ctx.wait(t, "cv")
                    ctx.inc("ct", t.matmul(pA[0:NE, :], ones_rb[0:1, 0:NE],
                                           rw1_row[0:1, o:o + 512], start=True, stop=True))
                    ctx.inc("ct", t.matmul(pB[0:NE, :], ones_row[0:1, 0:NE],
                                           rwch[0:1, :], start=True, stop=True))

                @blk.vector
                def _(v, o=o, nn=nn):
                    ctx.wait(v, "ct")
                    ctx.chain(v, "cv", v.tensor_mul(em_scr[:, :], m0_em[:, o:o + 512], pA[0:NE, :]))
                    ctx.chain(v, "cv", v.tensor_mul(em_scr2[:, :], m1_em[:, o:o + 512], pB[0:NE, :]))
                    ctx.chain(v, "cv", v.tensor_add(em_scr[:, :], em_scr[:, :], em_scr2[:, :]))
                    ctx.chain(v, "cv", v.scalar_tensor_tensor(
                        out=em_scr2[:, :], in0=em_scr[:, :], scalar=sid_b[:, 0:1],
                        in1=w_own[:, nn * 512:(nn + 1) * 512], op0=ALU.mult, op1=ALU.add))
                    ctx.chain(v, "cv", v.tensor_copy(out=w_own[:, nn * 512:(nn + 1) * 512],
                                                     in_=em_scr2[:, :]))

        @blk.sync
        def _(sy):
            ctx.wait(sy, "cv")
            ctx.dma("sl", sy.dma_start(out=w_rows_d.ap(), in_=w_own[:, :]))

        if debug:
            @blk.vector
            def _(v):
                ctx.wait(v, "cs", "ct")
                ctx.chain(v, "cv", v.tensor_copy(out=rk_blk[:, 0:2048], in_=m0_em[:, 0:2048]))

            @blk.sync
            def _(sy):
                ctx.wait(sy, "cv")
                ctx.dma("so", sy.dma_start(out=dbg_w, in_=w_own[:, :]))
            for b4 in range(4):
                @blk.vector
                def _(v, b4=b4):
                    ctx.wait(v, "so")
                    ctx.chain(v, "cv", v.tensor_copy(out=rk_blk[:, :], in_=m0_em[:, b4*2048:(b4+1)*2048]))

                @blk.sync
                def _(sy, b4=b4):
                    ctx.wait(sy, "cv")
                    ctx.dma("so", sy.dma_start(out=dbg_m0[:, b4*2048:(b4+1)*2048], in_=rk_blk[:, :]))
            for b4 in range(4):
                @blk.vector
                def _(v, b4=b4):
                    ctx.wait(v, "so")
                    ctx.chain(v, "cv", v.tensor_copy(out=rk_blk[:, :], in_=m1_em[:, b4*2048:(b4+1)*2048]))

                @blk.sync
                def _(sy, b4=b4):
                    ctx.wait(sy, "cv")
                    ctx.dma("so", sy.dma_start(out=dbg_m1[:, b4*2048:(b4+1)*2048], in_=rk_blk[:, :]))
            for b4 in range(4):
                @blk.vector
                def _(v, b4=b4):
                    ctx.wait(v, "so")
                    ctx.chain(v, "cv", v.tensor_copy(out=scr[0:1, 0:1024], in_=rw1_row[0:1, b4*2048:b4*2048+1024]))
                    ctx.chain(v, "cv", v.tensor_copy(out=scr2[0:1, 0:1024], in_=rw1_row[0:1, b4*2048+1024:(b4+1)*2048]))

                @blk.sync
                def _(sy, b4=b4):
                    ctx.wait(sy, "cv")
                    ctx.dma("so", sy.dma_start(out=dbg_rw[0:1, b4*2048:b4*2048+1024], in_=scr[0:1, 0:1024]))
                    ctx.dma("so", sy.dma_start(out=dbg_rw[0:1, b4*2048+1024:(b4+1)*2048], in_=scr2[0:1, 0:1024]))

        es_p2.close()
        es_p45 = ExitStack()
        x0_bf = es_p45.enter_context(guard("x0_bf", [128, NCH * TLOC], BF16))
        es_p4 = ExitStack()
        sqb = es_p4.enter_context(guard("sqb4", [128, NCH * TLOC], BF16))
        w = dict(pC=pC, pD=pD, ones_bf=ones_bf, ones_row=ones_row, sq=sqb,
                 row_mu=row_mu, row_e2=row_e2, row_r=row_r, mu_b=mu_b, r_b=r_b, scr=scr,
                 eps1=eps1)

        @blk.gpsimd
        def _(g):
            ctx.wait(g, "cv", "cs", "ct")
            ctx.dma("gl", g.dma_start(out=x0_bf.ap().rearrange("p (c t) -> p c t", t=TLOC),
                                      in_=x_own.rearrange("(c p) t -> p c t", p=128)))

        emit_ln(ctx, lambda ci: x0_bf[:, ci * TLOC:(ci + 1) * TLOC],
                lambda ci: xln[:, ci * TLOC:(ci + 1) * TLOC], w)

        es_p4.close()
        es_p5 = ExitStack()
        hprime = es_p5.enter_context(guard("hprime", [128, NE * 2 * TLOC], BF16))
        es_p5b = ExitStack()
        we_sb = es_p5b.enter_context(guard("we_sb", [128, 2 * NCH * FEXP], BF16))
        wb_e = es_p5b.enter_context(guard("wb_e", [128, TLOC]))
        w_row_sb = es_p5b.enter_context(guard("w_row_sb", [1, TLOC]))
        sg_sb = es_p5b.enter_context(guard("sg_sb", [128, TLOC]))
        su_sb = es_p5b.enter_context(guard("su_sb", [128, TLOC]))

        WE_SLOT = 2 * NCH * FEXP
        for e in range(NE):
            slot = 0
            base = 0

            @blk.gpsimd
            def _(g, e=e, slot=slot):
                ctx.wait(g, "ct", "sl")
                ctx.dma("gl", g.dma_start(
                    out=we_sb.ap()[:, slot * WE_SLOT: slot * WE_SLOT + NCH * FEXP]
                        .rearrange("p (c f) -> p c f", f=FEXP),
                    in_=w1T[e].rearrange("(c p) f -> p c f", p=128)))
                ctx.dma("gl", g.dma_start(
                    out=we_sb.ap()[:, slot * WE_SLOT + NCH * FEXP: (slot + 1) * WE_SLOT]
                        .rearrange("p (c f) -> p c f", f=FEXP),
                    in_=w3T[e].rearrange("(c p) f -> p c f", p=128)))
                ctx.dma("gl", g.dma_start(out=w_row_sb.ap(), in_=w_rows_d.ap()[e:e + 1, :]))

            @blk.tensor
            def _(t):
                ctx.wait(t, "gl", "cs", "cv")
                for nn in range(2):
                    ctx.inc("ct", t.matmul(pC[:, nn * 512:(nn + 1) * 512], ones_row[0:1, :],
                                           w_row_sb[0:1, nn * 512:(nn + 1) * 512],
                                           start=True, stop=True))

            @blk.scalar
            def _(s):
                ctx.wait(s, "ct")
                ctx.chain(s, "cs", s.copy(wb_e[:, :], pC[:, 0:TLOC]))

            for m2 in range(2):
                @blk.tensor
                def _(t, m2=m2, base=base):
                    ctx.wait(t, "cs", "cv")
                    for nn in range(2):
                        pdst = pA if nn == 0 else pB
                        for ci in range(NCH):
                            ctx.inc("ct", t.matmul(
                                pdst[:, 0:512],
                                we_sb[:, base + ci * FEXP + m2 * 128:
                                      base + ci * FEXP + m2 * 128 + 128],
                                xln[:, ci * TLOC + nn * 512: ci * TLOC + (nn + 1) * 512],
                                start=(ci == 0), stop=(ci == NCH - 1)))
                    for nn in range(2):
                        for ci in range(NCH):
                            ctx.inc("ct", t.matmul(
                                pD[:, nn * 512:(nn + 1) * 512],
                                we_sb[:, base + NCH * FEXP + ci * FEXP + m2 * 128:
                                      base + NCH * FEXP + ci * FEXP + m2 * 128 + 128],
                                xln[:, ci * TLOC + nn * 512: ci * TLOC + (nn + 1) * 512],
                                start=(ci == 0), stop=(ci == NCH - 1)))

                @blk.scalar
                def _(s):
                    ctx.wait(s, "ct")
                    ctx.chain(s, "cs", s.copy(sg_sb[:, 0:512], pA[:, 0:512]))
                    ctx.chain(s, "cs", s.copy(sg_sb[:, 512:TLOC], pB[:, 0:512]))
                    ctx.chain(s, "cs", s.copy(su_sb[:, :], pD[:, 0:TLOC]))

                @blk.vector
                def _(v):
                    ctx.wait(v, "cs")
                    ctx.chain(v, "cv", v.tensor_mul(scr[:, :], sg_sb[:, :], su_sb[:, :]))

                @blk.scalar
                def _(s):
                    ctx.wait(s, "cv")
                    ctx.chain(s, "cs", s.activation(out=scr2[:, :], in_=scr[:, :], func=AF.Silu))

                @blk.vector
                def _(v, e=e, m2=m2):
                    ctx.wait(v, "cs")
                    ctx.chain(v, "cv", v.tensor_mul(
                        hprime[:, (e * 2 + m2) * TLOC:(e * 2 + m2 + 1) * TLOC],
                        scr2[:, :], wb_e[:, :]))

        es_p5b.close()
        es_p6 = ExitStack()
        w2_sb = es_p6.enter_context(guard("w2_sb", [128, NE * 2 * 128], BF16))

        for mo in range(NCH):
            @blk.gpsimd
            def _(g, mo=mo):
                ctx.wait(g, "ct", "cv", "cs")
                for e in range(NE):
                    ctx.dma("gl", g.dma_start(
                        out=w2_sb.ap()[:, e * 256: e * 256 + 256]
                            .rearrange("p (k m) -> p k m", m=128),
                        in_=w2T[e].rearrange("(k p) m -> p k m", p=128)[:, :, mo * 128:(mo + 1) * 128]))

            @blk.sync
            def _(sy, mo=mo):
                ctx.wait(sy, "cv")
                ctx.dma("sl", sy.dma_start(
                    out=scr2[:, 0:TLOC],
                    in_=x_own.rearrange("(c p) t -> c p t", p=128)[mo]))

            @blk.tensor
            def _(t, mo=mo):
                ctx.wait(t, "gl", "cv")
                for nn in range(2):
                    pdst = pA if nn == 0 else pB
                    first = True
                    for e in range(NE):
                        for k2 in range(2):
                            ctx.inc("ct", t.matmul(
                                pdst[:, 0:512],
                                w2_sb[:, e * 256 + k2 * 128: e * 256 + k2 * 128 + 128],
                                hprime[:, (e * 2 + k2) * TLOC + nn * 512:
                                       (e * 2 + k2) * TLOC + nn * 512 + 512],
                                start=first, stop=(e == NE - 1 and k2 == 1)))
                            first = False

            @blk.vector
            def _(v, mo=mo):
                ctx.wait(v, "ct", "sl", "so")
                for nn in range(2):
                    pdst = pA if nn == 0 else pB
                    ctx.chain(v, "cv", v.scalar_tensor_tensor(
                        out=scr[:, nn * 512:(nn + 1) * 512],
                        in0=pdst[:, 0:512], scalar=0.5,
                        in1=scr2[:, nn * 512:(nn + 1) * 512],
                        op0=ALU.mult, op1=ALU.add))
                ctx.chain(v, "cv", v.tensor_copy(out=x0_bf[:, mo * TLOC:(mo + 1) * TLOC],
                                                 in_=scr[:, :]))

            @blk.sync
            def _(sy, mo=mo):
                ctx.wait(sy, "cv")
                ctx.dma("so", sy.dma_start(
                    out=x1_out.rearrange("(c p) t -> c p t", p=128)[mo], in_=scr[:, :]))

        es_p6.close()
        es_p5.close()
        es_p7 = ExitStack()
        sqb = es_p7.enter_context(guard("sqb7", [128, NCH * TLOC], BF16))
        w = dict(pC=pC, pD=pD, ones_bf=ones_bf, ones_row=ones_row, sq=sqb,
                 row_mu=row_mu, row_e2=row_e2, row_r=row_r, mu_b=mu_b, r_b=r_b, scr=scr,
                 eps1=eps1)

        emit_ln(ctx, lambda ci: x0_bf[:, ci * TLOC:(ci + 1) * TLOC],
                lambda ci: xln[:, ci * TLOC:(ci + 1) * TLOC], w)

        if debug:
            @blk.sync
            def _(sy):
                ctx.wait(sy, "cv", "cs")
                for ci in range(NCH):
                    ctx.dma("so", sy.dma_start(
                        out=dbg_xln.rearrange("(c p) t -> c p t", p=128)[ci],
                        in_=xln[:, ci * TLOC:(ci + 1) * TLOC]))

        es_p7.close()
        es_p45.close()
        es_p8 = ExitStack()
        wcol = es_p8.enter_context(guard("wcol", [128, 2 * NCH * 128], BF16))
        pchunk = es_p8.enter_context(guard("pchunk", [128, TLOC]))
        t1ch = es_p8.enter_context(guard("t1ch", [128, TLOC]))
        och = es_p8.enter_context(guard("och", [128, TLOC], BF16))

        for wi, (wdram, odram, kind) in enumerate(
                [(wqT, q_out, "rope"), (wkT, k_out, "rope"), (wvT, v_out, "copy")]):
            for mo in range(NCH):
                slot = (wi * NCH + mo) % 2
                cbase = slot * NCH * 128

                @blk.gpsimd
                def _(g, wdram=wdram, mo=mo, cbase=cbase):
                    ctx.wait(g, "ct", "cv", "cs")
                    ctx.dma("gl", g.dma_start(
                        out=wcol.ap()[:, cbase:cbase + NCH * 128]
                            .rearrange("p (c m) -> p c m", m=128),
                        in_=wdram.rearrange("(c p) m -> p c m", p=128)[:, :, mo * 128:(mo + 1) * 128]))

                @blk.tensor
                def _(t, cbase=cbase):
                    ctx.wait(t, "gl", "cs", "cv")
                    for nn in range(2):
                        pdst = pA if nn == 0 else pB
                        for ci in range(NCH):
                            ctx.inc("ct", t.matmul(
                                pdst[:, 0:512], wcol[:, cbase + ci * 128:cbase + (ci + 1) * 128],
                                xln[:, ci * TLOC + nn * 512: ci * TLOC + (nn + 1) * 512],
                                start=(ci == 0), stop=(ci == NCH - 1)))

                if kind == "rope":
                    @blk.scalar
                    def _(s):
                        ctx.wait(s, "ct", "cv", "so")
                        ctx.chain(s, "cs", s.copy(pchunk[:, 0:512], pA[:, 0:512]))
                        ctx.chain(s, "cs", s.copy(pchunk[:, 512:TLOC], pB[:, 0:512]))

                    emit_rope_L0(ctx, pchunk, och, cc_t, ss_t, pC, t1ch)
                else:
                    @blk.scalar
                    def _(s):
                        ctx.wait(s, "ct", "cv", "so")
                        ctx.chain(s, "cs", s.copy(och[:, 0:512], pA[:, 0:512]))
                        ctx.chain(s, "cs", s.copy(och[:, 512:TLOC], pB[:, 0:512]))

                @blk.sync
                def _(sy, odram=odram, mo=mo, kind=kind):
                    ctx.wait(sy, "cv" if kind == "rope" else "cs")
                    ctx.dma("so", sy.dma_start(out=odram[mo * 128:(mo + 1) * 128, :],
                                               in_=och[:, :]))

        @blk.sync
        def _(sy):
            ctx.wait(sy, "so")

        es_p8.close()

    ctx.es.close()
    return ctx





def build_launch_B():
    ctx = Ctx("B")
    nc = ctx.nc

    x1_in = nc.dram_tensor("x1_in", [D, TLOC], F32, kind="ExternalInput").ap()
    q_in = nc.dram_tensor("q_in", [D, TLOC], BF16, kind="ExternalInput").ap()
    k_in = nc.dram_tensor("k_in", [D, SEQ], BF16, kind="ExternalInput").ap()
    vt_in = nc.dram_tensor("vt_in", [SEQ, D], BF16, kind="ExternalInput").ap()
    woT = nc.dram_tensor("woT", [D, D], F32, kind="ExternalInput").ap()
    gamma_in = nc.dram_tensor("gamma_in", [1, 128], F32, kind="ExternalInput").ap()
    lq1_in = nc.dram_tensor("lq1_in", [1, 64], F32, kind="ExternalInput").ap()
    lk1_in = nc.dram_tensor("lk1_in", [1, 64], F32, kind="ExternalInput").ap()
    lq2_in = nc.dram_tensor("lq2_in", [1, 64], F32, kind="ExternalInput").ap()
    lk2_in = nc.dram_tensor("lk2_in", [1, 64], F32, kind="ExternalInput").ap()
    gate_wT = nc.dram_tensor("gate_wT", [D, NE], F32, kind="ExternalInput").ap()
    ident_in = nc.dram_tensor("ident_in", [128, 128], F32, kind="ExternalInput").ap()

    x2_out = nc.dram_tensor("x2_out", [D, TLOC], F32, kind="ExternalOutput").ap()
    x2b_out = nc.dram_tensor("x2b_out", [D, TLOC], BF16, kind="ExternalOutput").ap()
    cnt_out = nc.dram_tensor("cnt_out", [1, 2 * NE], F32, kind="ExternalOutput").ap()

    sb, ps = ctx.sb, ctx.ps
    ones_bf = sb("ones_bf", [128, 1], BF16)
    ones_row = sb("ones_row", [1, 128])
    ident = sb("ident", [128, 128])
    gate_sb = sb("gate_sb", [128, NCH * NE], BF16)
    gamma_r = sb("gamma_r", [1, 128])
    lrow = sb("lrow", [1, 64])
    lrow2 = sb("lrow2", [1, 64])
    lam_c = sb("lam_c", [128, 1])
    nlam_c = sb("nlam_c", [128, 1])
    sc1 = sb("sc1", [1, 1])
    sc2 = sb("sc2", [1, 1])
    row_mu = sb("row_mu", [1, TLOC])
    row_e2 = sb("row_e2", [1, TLOC])
    row_r = sb("row_r", [1, TLOC])
    mu_b = sb("mu_b", [128, TLOC])
    r_b = sb("r_b", [128, TLOC])
    scr = sb("scr", [128, TLOC])
    scr2 = sb("scr2", [128, TLOC])
    eps1 = sb("eps1", [1, 1])
    lg_tm = sb("lg_tm", [128, NE])
    m_tm = sb("m_tm", [128, NE])
    mt_tm = sb("mt_tm", [128, NE])
    l1_c = sb("l1_c", [128, 1])
    l2_c = sb("l2_c", [128, 1])
    cnt01 = sb("cnt01", [NE, 2])
    cnt_r = sb("cnt_r", [1, 2 * NE])
    m8a = sb("m8a", [128, 8])
    s8a = sb("s8a", [128, 8])
    m8b = sb("m8b", [128, 8])
    s8b = sb("s8b", [128, 8])
    mq4 = sb("mq4", [128, 4])
    sq4 = sb("sq4", [128, 4])
    tcol = sb("tcol", [128, 4])
    mcol = sb("mcol", [128, 1])
    nmcol = sb("nmcol", [128, 1])
    f8 = sb("f8", [128, 8])
    f_row = sb("f_row", [1, TLOC])
    f_rowb = sb("f_rowb", [1, TLOC])
    fb1 = sb("fb1", [128, 512])
    fb2 = sb("fb2", [128, 512])
    c1c = sb("c1c", [128, 1])
    c2c = sb("c2c", [128, 1])
    phat1 = sb("phat1", [128, 512], BF16)
    phat2 = sb("phat2", [128, 512], BF16)
    rrow = sb("rrow", [1, 128])

    def guard(name, shape, dt=F32):
        return nc.sbuf_tensor(name, shape, dt)

    pS1 = ps("pS1", [128, TLOC])
    pS2 = ps("pS2", [128, TLOC])
    pO1 = ps("pO1", [128, 512])
    pO2 = ps("pO2", [128, 512])
    pX = ps("pX", [128, 512])
    pY = ps("pY", [128, 512])

    w = dict(pC=pS1, pD=pS2, ones_bf=ones_bf, ones_row=ones_row,
             row_mu=row_mu, row_e2=row_e2, row_r=row_r, mu_b=mu_b, r_b=r_b,
             scr=scr, eps1=eps1)

    with nc.Block() as blk:
        ctx.blk = blk

        @blk.sync
        def _(sy):
            ctx.dma("sl", sy.dma_start(out=ident[:], in_=ident_in))
            ctx.dma("sl", sy.dma_start(out=gamma_r[:], in_=gamma_in))
            ctx.dma("sl", sy.dma_start(out=lrow[:], in_=lq1_in))
            ctx.dma("sl", sy.dma_start(out=lrow2[:], in_=lk1_in))

        @blk.gpsimd
        def _(g):
            ctx.dma("gl", g.dma_start(out=gate_sb.ap().rearrange("p (c e) -> p c e", e=NE),
                                      in_=gate_wT.rearrange("(c p) e -> p c e", p=128)))

        @blk.vector
        def _(v):
            ctx.inc("cv", v.memset(ones_bf[:], 1.0))
            ctx.chain(v, "cv", v.memset(ones_row[:], 1.0))
            ctx.chain(v, "cv", v.memset(eps1[:, :], LN_EPS))
            ctx.chain(v, "cv", v.memset(cnt01[:, :], 0.0))

        for pi, (qa, ka, dst) in enumerate([(lq1_in, lk1_in, sc1), (lq2_in, lk2_in, sc2)]):
            @blk.sync
            def _(sy, qa=qa, ka=ka):
                ctx.wait(sy, "cv", "cs")
                ctx.dma("sl", sy.dma_start(out=lrow[:], in_=qa))
                ctx.dma("sl", sy.dma_start(out=lrow2[:], in_=ka))

            @blk.vector
            def _(v, dst=dst):
                ctx.wait(v, "sl")
                ctx.chain(v, "cv", v.tensor_mul(lrow[:, :], lrow[:, :], lrow2[:, :]))
                ctx.chain(v, "cv", v.reduce_sum(dst[:, :], lrow[:, :], axis=AX.X))

        @blk.scalar
        def _(s):
            ctx.wait(s, "cv")
            ctx.chain(s, "cs", s.activation(out=sc1[:, :], in_=sc1[:, :], func=AF.Exp))
            ctx.chain(s, "cs", s.activation(out=sc2[:, :], in_=sc2[:, :], func=AF.Exp))

        @blk.vector
        def _(v):
            ctx.wait(v, "cs")
            ctx.chain(v, "cv", v.tensor_sub(sc1[:, :], sc1[:, :], sc2[:, :]))
            ctx.chain(v, "cv", v.tensor_scalar(out=sc1[:, :], in0=sc1[:, :],
                                               scalar1=LAMBDA_INIT, scalar2=None, op0=ALU.add))

        @blk.tensor
        def _(t):
            ctx.wait(t, "cv")
            ctx.inc("ct", t.matmul(pX[:, 0:1], ones_row[0:1, :], sc1[0:1, 0:1],
                                   start=True, stop=True))

        @blk.scalar
        def _(s):
            ctx.wait(s, "ct")
            ctx.chain(s, "cs", s.copy(lam_c[:, :], pX[:, 0:1]))

        @blk.vector
        def _(v):
            ctx.wait(v, "cs")
            ctx.chain(v, "cv", v.tensor_scalar(out=nlam_c[:, :], in0=lam_c[:, :],
                                               scalar1=-1.0, scalar2=None, op0=ALU.mult))

        es_ob = ExitStack()
        obuf = es_ob.enter_context(guard("obuf", [128, NCH * TLOC], BF16))
        es_at = ExitStack()
        q_sb = es_at.enter_context(guard("q_sb", [128, NCH * TLOC], BF16))
        kh = es_at.enter_context(guard("kh", [128, 2 * SEQ], BF16))
        vh = es_at.enter_context(guard("vh", [128, 2 * SEQ], BF16))
        exp_scr = es_at.enter_context(guard("exp_scr", [128, TLOC]))
        ones_cf = es_at.enter_context(guard("ones_cf", [128, 1]))

        @blk.vector
        def _(v):
            ctx.chain(v, "cv", v.memset(ones_cf[:, :], 1.0))

        @blk.sync
        def _(sy):
            ctx.wait(sy, "cv")
            ctx.dma("sl", sy.dma_start(out=q_sb.ap().rearrange("p (c t) -> p c t", t=TLOC),
                                       in_=q_in.rearrange("(c p) t -> p c t", p=128)))

        for h in range(16):
            slot = h % 2
            kb = slot * SEQ

            @blk.sync
            def _(sy, h=h, slot=slot):
                ctx.wait(sy, "ct", "cs")
                ctx.dma("sl", sy.dma_start(out=kh.ap()[:, slot * SEQ:(slot + 1) * SEQ],
                                           in_=k_in[h * 128:(h + 1) * 128, :]))
                ctx.dma("sl", sy.dma_start(
                    out=vh.ap()[:, slot * SEQ:(slot + 1) * SEQ].rearrange("p (kt d) -> p kt d", d=128),
                    in_=vt_in.rearrange("(kt p) d -> p kt d", p=128)[:, :, h * 128:(h + 1) * 128]))

            for tau in range(8):
                for qq in range(4):
                    @blk.tensor
                    def _(t, h=h, tau=tau, qq=qq, kb=kb):
                        ctx.wait(t, "sl", "cs", "cv")
                        for vi in range(2):
                            vb = 64 * vi
                            pS = pS1 if vi == 0 else pS2
                            for kc in range(2):
                                ctx.inc("ct", t.matmul(
                                    pS[:, kc * 512:(kc + 1) * 512],
                                    q_sb[vb:vb + 64, h * TLOC + tau * 128: h * TLOC + (tau + 1) * 128],
                                    kh[vb:vb + 64, kb + qq * TLOC + kc * 512: kb + qq * TLOC + (kc + 1) * 512],
                                    start=True, stop=True, tile_position=(vb, 0)))

                    @blk.vector
                    def _(v, qq=qq):
                        ctx.wait(v, "ct")
                        ctx.chain(v, "cv", v.reduce_max(mq4[:, qq:qq + 1], pS1[:, 0:TLOC], axis=AX.X))
                        ctx.chain(v, "cv", v.reduce_max(tcol[:, qq:qq + 1], pS2[:, 0:TLOC], axis=AX.X))
                        ctx.chain(v, "cv", v.tensor_scalar(out=mcol[:, :], in0=mq4[:, qq:qq + 1],
                                                           scalar1=-0.125, scalar2=None, op0=ALU.mult))
                        ctx.chain(v, "cv", v.tensor_scalar(out=nmcol[:, :], in0=tcol[:, qq:qq + 1],
                                                           scalar1=-0.125, scalar2=None, op0=ALU.mult))

                    @blk.scalar
                    def _(s, qq=qq):
                        ctx.wait(s, "cv")
                        ctx.chain(s, "cs", s.activation(out=exp_scr[:, :], in_=pS1[:, 0:TLOC],
                                                        func=AF.Exp, scale=0.125, bias=mcol[:, 0:1],
                                                        accum_out=sq4[:, qq:qq + 1]))
                        ctx.chain(s, "cs", s.activation(out=exp_scr[:, :], in_=pS2[:, 0:TLOC],
                                                        func=AF.Exp, scale=0.125, bias=nmcol[:, 0:1],
                                                        accum_out=s8b[:, 7:8]))

                    @blk.vector
                    def _(v, qq=qq):
                        ctx.wait(v, "cs")
                        ctx.chain(v, "cv", v.tensor_copy(out=sq4[:, qq:qq + 1], in_=sq4[:, qq:qq + 1]))
                        ctx.chain(v, "cv", v.tensor_copy(out=f8[:, qq:qq + 1], in_=s8b[:, 7:8]))

                @blk.vector
                def _(v, tau=tau):
                    ctx.wait(v, "cs")
                    ctx.chain(v, "cv", v.reduce_max(mcol[:, :], mq4[:, :], axis=AX.X))
                    ctx.chain(v, "cv", v.tensor_scalar(out=m8a[:, tau:tau + 1], in0=mcol[:, :],
                                                       scalar1=0.125, scalar2=None, op0=ALU.mult))
                    ctx.chain(v, "cv", v.tensor_scalar(out=mq4[:, :], in0=mq4[:, :],
                                                       scalar1=mcol[:, 0:1], scalar2=None, op0=ALU.subtract))
                    ctx.chain(v, "cv", v.reduce_max(nmcol[:, :], tcol[:, :], axis=AX.X))
                    ctx.chain(v, "cv", v.tensor_scalar(out=m8b[:, tau:tau + 1], in0=nmcol[:, :],
                                                       scalar1=0.125, scalar2=None, op0=ALU.mult))
                    ctx.chain(v, "cv", v.tensor_scalar(out=tcol[:, :], in0=tcol[:, :],
                                                       scalar1=nmcol[:, 0:1], scalar2=None, op0=ALU.subtract))

                @blk.scalar
                def _(s):
                    ctx.wait(s, "cv")
                    ctx.chain(s, "cs", s.activation(out=mq4[:, :], in_=mq4[:, :], func=AF.Exp, scale=0.125))
                    ctx.chain(s, "cs", s.activation(out=tcol[:, :], in_=tcol[:, :], func=AF.Exp, scale=0.125))

                @blk.vector
                def _(v, tau=tau):
                    ctx.wait(v, "cs")
                    ctx.chain(v, "cv", v.tensor_mul(mq4[:, :], mq4[:, :], sq4[:, :]))
                    ctx.chain(v, "cv", v.reduce_sum(s8a[:, tau:tau + 1], mq4[:, :], axis=AX.X))
                    ctx.chain(v, "cv", v.tensor_mul(tcol[:, :], tcol[:, :], f8[:, 0:4]))
                    ctx.chain(v, "cv", v.reduce_sum(s8b[:, tau:tau + 1], tcol[:, :], axis=AX.X))

            for vi, (m8, s8, cC, nC, frow) in enumerate(
                    [(m8a, s8a, c1c, None, f_row), (m8b, s8b, c2c, None, f_rowb)]):
                @blk.vector
                def _(v, m8=m8):
                    ctx.wait(v, "cs", "ct")
                    ctx.chain(v, "cv", v.reduce_max(mcol[:, :], m8[:, :], axis=AX.X))

                @blk.tensor
                def _(t):
                    ctx.wait(t, "cv")
                    ctx.inc("ct", t.matmul(pX[0:1, 0:128], mcol[:, 0:1], ident[:],
                                           start=True, stop=True))

                @blk.scalar
                def _(s):
                    ctx.wait(s, "ct")
                    ctx.chain(s, "cs", s.copy(rrow[:, :], pX[0:1, 0:128]))

                @blk.vector
                def _(v):
                    ctx.wait(v, "cs")
                    ctx.chain(v, "cv", v.reduce_max(sc1[:, :], rrow[:, :], axis=AX.X))

                @blk.tensor
                def _(t):
                    ctx.wait(t, "cv")
                    ctx.inc("ct", t.matmul(pX[:, 0:1], ones_row[0:1, :], sc1[0:1, 0:1],
                                           start=True, stop=True))

                @blk.scalar
                def _(s, cC=cC, m8=m8, s8=s8):
                    ctx.wait(s, "ct")
                    ctx.chain(s, "cs", s.copy(cC[:, :], pX[:, 0:1]))
                    ctx.chain(s, "cs", s.activation(out=f8[:, :], in_=s8[:, :], func=AF.Ln))

                @blk.vector
                def _(v, m8=m8):
                    ctx.wait(v, "cs")
                    ctx.chain(v, "cv", v.tensor_add(f8[:, :], f8[:, :], m8[:, :]))

                @blk.scalar
                def _(s, cC=cC):
                    ctx.wait(s, "cv")
                    ctx.chain(s, "cs", s.activation(out=f8[:, :], in_=f8[:, :], func=AF.Exp,
                                                    scale=-1.0, bias=cC[:, 0:1]))

                for tau in range(8):
                    @blk.tensor
                    def _(t, tau=tau):
                        ctx.wait(t, "cs")
                        ctx.inc("ct", t.matmul(pX[0:1, 0:128], f8[:, tau:tau + 1], ident[:],
                                               start=True, stop=True))

                    @blk.scalar
                    def _(s, tau=tau, frow=frow):
                        ctx.wait(s, "ct")
                        ctx.chain(s, "cs", s.copy(frow[0:1, tau * 128:(tau + 1) * 128],
                                                  pX[0:1, 0:128]))

            @blk.vector
            def _(v):
                ctx.wait(v, "cs")
                ctx.chain(v, "cv", v.tensor_scalar(out=c1c[:, :], in0=c1c[:, :],
                                                   scalar1=-1.0, scalar2=None, op0=ALU.mult))
                ctx.chain(v, "cv", v.tensor_scalar(out=c2c[:, :], in0=c2c[:, :],
                                                   scalar1=-1.0, scalar2=None, op0=ALU.mult))

            for qc in range(2):
                @blk.tensor
                def _(t, qc=qc):
                    ctx.wait(t, "cv", "cs")
                    ctx.inc("ct", t.matmul(pX[:, 0:512], ones_row[0:1, :],
                                           f_row[0:1, qc * 512:(qc + 1) * 512], start=True, stop=True))
                    ctx.inc("ct", t.matmul(pY[:, 0:512], ones_row[0:1, :],
                                           f_rowb[0:1, qc * 512:(qc + 1) * 512], start=True, stop=True))

                @blk.scalar
                def _(s):
                    ctx.wait(s, "ct")
                    ctx.chain(s, "cs", s.copy(fb1[:, :], pX[:, 0:512]))
                    ctx.chain(s, "cs", s.copy(fb2[:, :], pY[:, 0:512]))

                for kt in range(32):
                    @blk.tensor
                    def _(t, h=h, qc=qc, kt=kt, kb=kb):
                        ctx.wait(t, "cs", "cv")
                        for vi in range(2):
                            vb = 64 * vi
                            pS = pS1 if vi == 0 else pS2
                            ctx.inc("ct", t.matmul(
                                pS[:, 0:512],
                                kh[vb:vb + 64, kb + kt * 128: kb + (kt + 1) * 128],
                                q_sb[vb:vb + 64, h * TLOC + qc * 512: h * TLOC + (qc + 1) * 512],
                                start=True, stop=True, tile_position=(vb, 0)))

                    @blk.scalar
                    def _(s, kt=kt):
                        ctx.wait(s, "ct")
                        ctx.chain(s, "cs", s.activation(out=phat1[:, :], in_=pS1[:, 0:512],
                                                        func=AF.Exp, scale=0.125, bias=c1c[:, 0:1]))
                        ctx.chain(s, "cs", s.activation(out=phat2[:, :], in_=pS2[:, 0:512],
                                                        func=AF.Exp, scale=0.125, bias=c2c[:, 0:1]))

                    @blk.tensor
                    def _(t, kt=kt, kb=kb):
                        ctx.wait(t, "cs")
                        ctx.inc("ct", t.matmul(pO1[:, 0:512],
                                               vh[:, kb + kt * 128: kb + (kt + 1) * 128],
                                               phat1[:, :], start=(kt == 0), stop=(kt == 31)))
                        ctx.inc("ct", t.matmul(pO2[:, 0:512],
                                               vh[:, kb + kt * 128: kb + (kt + 1) * 128],
                                               phat2[:, :], start=(kt == 0), stop=(kt == 31)))

                @blk.vector
                def _(v):
                    ctx.wait(v, "ct")
                    ctx.chain(v, "cv", v.tensor_mul(scr[:, 0:512], pO1[:, 0:512], fb1[:, :]))
                    ctx.chain(v, "cv", v.tensor_mul(scr2[:, 0:512], pO2[:, 0:512], fb2[:, :]))
                    ctx.chain(v, "cv", v.scalar_tensor_tensor(out=scr[:, 512:TLOC],
                                                              in0=scr2[:, 0:512], scalar=nlam_c[:, 0:1],
                                                              in1=scr[:, 0:512],
                                                              op0=ALU.mult, op1=ALU.add))

                @blk.scalar
                def _(s):
                    ctx.wait(s, "cv")
                    ctx.chain(s, "cs", s.activation(out=exp_scr[:, 0:512], in_=scr[:, 512:TLOC],
                                                    func=AF.Square))

                @blk.tensor
                def _(t):
                    ctx.wait(t, "cs")
                    ctx.inc("ct", t.matmul(pX[0:1, 0:512], ones_cf[:, 0:1], exp_scr[:, 0:512],
                                           start=True, stop=True))

                @blk.scalar
                def _(s):
                    ctx.wait(s, "ct")
                    ctx.chain(s, "cs", s.activation(out=row_r[0:1, 0:512], in_=pX[0:1, 0:512],
                                                    func=AF.Sqrt, scale=1.0 / 128.0, bias=eps1[0:1, 0:1]))

                @blk.vector
                def _(v):
                    ctx.wait(v, "cs")
                    ctx.chain(v, "cv", v.reciprocal(row_r[0:1, 0:512], row_r[0:1, 0:512]))
                    ctx.chain(v, "cv", v.tensor_scalar(out=row_r[0:1, 0:512], in0=row_r[0:1, 0:512],
                                                       scalar1=1.0 - LAMBDA_INIT, scalar2=None,
                                                       op0=ALU.mult))

                @blk.tensor
                def _(t):
                    ctx.wait(t, "cv")
                    ctx.inc("ct", t.matmul(pY[:, 0:512], gamma_r[0:1, :], row_r[0:1, 0:512],
                                           start=True, stop=True))

                @blk.vector
                def _(v, h=h, qc=qc):
                    ctx.wait(v, "ct")
                    ctx.chain(v, "cv", v.tensor_mul(obuf[:, h * TLOC + qc * 512: h * TLOC + (qc + 1) * 512],
                                                    scr[:, 512:TLOC], pY[:, 0:512]))

        es_at.close()
        es_p3 = ExitStack()
        wcol = es_p3.enter_context(guard("wcol", [128, NCH * 128], BF16))
        x2_bf = es_p3.enter_context(guard("x2_bf", [128, NCH * TLOC], BF16))

        for mo in range(NCH):
            @blk.gpsimd
            def _(g, mo=mo):
                ctx.wait(g, "ct", "cv", "cs")
                ctx.dma("gl", g.dma_start(
                    out=wcol.ap().rearrange("p (c m) -> p c m", m=128),
                    in_=woT.rearrange("(c p) m -> p c m", p=128)[:, :, mo * 128:(mo + 1) * 128]))

            @blk.sync
            def _(sy, mo=mo):
                ctx.wait(sy, "cv")
                ctx.dma("sl", sy.dma_start(
                    out=scr2[:, 0:TLOC],
                    in_=x1_in.rearrange("(c p) t -> c p t", p=128)[mo]))

            @blk.tensor
            def _(t, mo=mo):
                ctx.wait(t, "gl", "cv", "cs")
                for nn in range(2):
                    pdst = pO1 if nn == 0 else pO2
                    for ci in range(NCH):
                        ctx.inc("ct", t.matmul(
                            pdst[:, 0:512], wcol[:, ci * 128:(ci + 1) * 128],
                            obuf[:, ci * TLOC + nn * 512: ci * TLOC + (nn + 1) * 512],
                            start=(ci == 0), stop=(ci == NCH - 1)))

            @blk.vector
            def _(v, mo=mo):
                ctx.wait(v, "ct", "sl", "so")
                for nn in range(2):
                    pdst = pO1 if nn == 0 else pO2
                    ctx.chain(v, "cv", v.tensor_add(
                        scr[:, nn * 512:(nn + 1) * 512], pdst[:, 0:512],
                        scr2[:, nn * 512:(nn + 1) * 512]))
                ctx.chain(v, "cv", v.tensor_copy(out=x2_bf[:, mo * TLOC:(mo + 1) * TLOC],
                                                 in_=scr[:, 0:TLOC]))

            @blk.sync
            def _(sy, mo=mo):
                ctx.wait(sy, "cv")
                ctx.dma("so", sy.dma_start(
                    out=x2_out.rearrange("(c p) t -> c p t", p=128)[mo], in_=scr[:, 0:TLOC]))

        es_p4 = ExitStack()
        xlnb = es_p4.enter_context(guard("xlnb", [128, NCH * TLOC], BF16))
        sqb = es_p4.enter_context(guard("sqb", [128, NCH * TLOC], BF16))
        w["sq"] = sqb

        emit_ln(ctx, lambda ci: x2_bf[:, ci * TLOC:(ci + 1) * TLOC],
                lambda ci: xlnb[:, ci * TLOC:(ci + 1) * TLOC], w)

        for tau in range(8):
            @blk.tensor
            def _(t, tau=tau):
                ctx.wait(t, "cv", "cs")
                for ci in range(NCH):
                    ctx.inc("ct", t.matmul(
                        pX[:, 0:NE],
                        xlnb[:, ci * TLOC + tau * 128: ci * TLOC + (tau + 1) * 128],
                        gate_sb[:, ci * NE:(ci + 1) * NE],
                        start=(ci == 0), stop=(ci == NCH - 1)))

            @blk.scalar
            def _(s):
                ctx.wait(s, "ct")
                ctx.chain(s, "cs", s.copy(lg_tm[:, :], pX[:, 0:NE]))

            @blk.vector
            def _(v):
                ctx.wait(v, "cs")
                ctx.chain(v, "cv", v.reduce_max(l1_c[:, :], lg_tm[:, :], axis=AX.X))
                ctx.chain(v, "cv", v.tensor_scalar(out=m_tm[:, :], in0=lg_tm[:, :],
                                                   scalar1=l1_c[:, 0:1], scalar2=None,
                                                   op0=ALU.is_equal))
                ctx.chain(v, "cv", v.scalar_tensor_tensor(out=mt_tm[:, :], in0=m_tm[:, :],
                                                          scalar=NEG, in1=lg_tm[:, :],
                                                          op0=ALU.mult, op1=ALU.add))
                ctx.chain(v, "cv", v.reduce_max(l2_c[:, :], mt_tm[:, :], axis=AX.X))
                ctx.chain(v, "cv", v.tensor_scalar(out=mt_tm[:, :], in0=lg_tm[:, :],
                                                   scalar1=l2_c[:, 0:1], scalar2=None,
                                                   op0=ALU.is_equal))

            @blk.tensor
            def _(t):
                ctx.wait(t, "cv")
                ctx.inc("ct", t.transpose(pY[0:NE, 0:128], m_tm[:, 0:NE], ident[:]))

            @blk.scalar
            def _(s):
                ctx.wait(s, "ct")
                ctx.chain(s, "cs", s.copy(scr[0:NE, 0:128], pY[0:NE, 0:128]))

            @blk.tensor
            def _(t):
                ctx.wait(t, "cs")
                ctx.inc("ct", t.transpose(pY[0:NE, 0:128], mt_tm[:, 0:NE], ident[:]))

            @blk.scalar
            def _(s):
                ctx.wait(s, "ct")
                ctx.chain(s, "cs", s.copy(scr2[0:NE, 0:128], pY[0:NE, 0:128]))

            @blk.vector
            def _(v):
                ctx.wait(v, "cs")
                ctx.chain(v, "cv", v.reduce_sum(mcol[0:NE, 0:1], scr[0:NE, 0:128], axis=AX.X))
                ctx.chain(v, "cv", v.tensor_add(cnt01[:, 0:1], cnt01[:, 0:1], mcol[0:NE, 0:1]))
                ctx.chain(v, "cv", v.reduce_sum(mcol[0:NE, 0:1], scr2[0:NE, 0:128], axis=AX.X))
                ctx.chain(v, "cv", v.tensor_add(cnt01[:, 1:2], cnt01[:, 1:2], mcol[0:NE, 0:1]))

        for sl_ in range(2):
            @blk.tensor
            def _(t, sl_=sl_):
                ctx.wait(t, "cv", "cs")
                ctx.inc("ct", t.matmul(pX[0:1, 0:NE], cnt01[:, sl_:sl_ + 1], ident[0:NE, 0:NE],
                                       start=True, stop=True))

            @blk.scalar
            def _(s, sl_=sl_):
                ctx.wait(s, "ct")
                ctx.chain(s, "cs", s.copy(cnt_r[0:1, sl_ * NE:(sl_ + 1) * NE], pX[0:1, 0:NE]))

        @blk.sync
        def _(sy):
            ctx.wait(sy, "cs")
            ctx.dma("so", sy.dma_start(out=cnt_out, in_=cnt_r[:, :]))
            ctx.dma("so", sy.dma_start(out=x2b_out.rearrange("(c p) t -> p c t", p=128),
                                       in_=x2_bf.ap().rearrange("p (c t) -> p c t", t=TLOC)))

        @blk.sync
        def _(sy):
            ctx.wait(sy, "so")

        es_p4.close()
        es_p3.close()
        es_ob.close()

    ctx.es.close()
    return ctx





INTER = 8192
KVD = 512


def build_launch_C(debug=False):
    ctx = Ctx("C")
    nc = ctx.nc

    x2_in = nc.dram_tensor("x2_in", [D, TLOC], F32, kind="ExternalInput").ap()
    x2b_in = nc.dram_tensor("x2b_in", [D, TLOC], BF16, kind="ExternalInput").ap()
    cnts_in = nc.dram_tensor("cnts_in", [8, 2 * NE], F32, kind="ExternalInput").ap()
    selp_in = nc.dram_tensor("selp_in", [8, 1], F32, kind="ExternalInput").ap()
    gate_wT = nc.dram_tensor("gate_wT", [D, NE], F32, kind="ExternalInput").ap()
    w1T = nc.dram_tensor("w1T", [NE, D, FEXP], F32, kind="ExternalInput").ap()
    w3T = nc.dram_tensor("w3T", [NE, D, FEXP], F32, kind="ExternalInput").ap()
    w2T = nc.dram_tensor("w2T", [NE, FEXP, D], F32, kind="ExternalInput").ap()
    wguT = nc.dram_tensor("wguT", [D, 2 * INTER], F32, kind="ExternalInput").ap()
    wdT = nc.dram_tensor("wdT", [INTER, D], F32, kind="ExternalInput").ap()
    wqT = nc.dram_tensor("wqT", [D, D], F32, kind="ExternalInput").ap()
    wkT = nc.dram_tensor("wkT", [D, KVD], F32, kind="ExternalInput").ap()
    wvT = nc.dram_tensor("wvT", [D, KVD], F32, kind="ExternalInput").ap()
    gq_r = nc.dram_tensor("gq_r", [1, D], F32, kind="ExternalInput").ap()
    gk_r = nc.dram_tensor("gk_r", [1, D], F32, kind="ExternalInput").ap()
    gv_r = nc.dram_tensor("gv_r", [1, D], F32, kind="ExternalInput").ap()
    pos_in = nc.dram_tensor("pos", [1, TLOC], F32, kind="ExternalInput").ap()
    ident_in = nc.dram_tensor("ident_in", [128, 128], F32, kind="ExternalInput").ap()

    x4_out = nc.dram_tensor("x4_out", [D, TLOC], F32, kind="ExternalOutput").ap()
    q_out = nc.dram_tensor("q_out", [D, TLOC], BF16, kind="ExternalOutput").ap()
    k_out = nc.dram_tensor("k_out", [KVD, TLOC], BF16, kind="ExternalOutput").ap()
    v_out = nc.dram_tensor("v_out", [KVD, TLOC], BF16, kind="ExternalOutput").ap()

    w_rows_d = nc.dram_tensor("w_rows_d", [NE, TLOC], F32, kind="Internal")
    x3_d = nc.dram_tensor("x3_d", [D, TLOC], F32, kind="Internal")
    if debug:
        dbg_x3 = nc.dram_tensor("dbg_x3", [D, TLOC], F32, kind="ExternalOutput").ap()
        dbg_w = nc.dram_tensor("dbg_w", [NE, TLOC], F32, kind="ExternalOutput").ap()

    sb, ps = ctx.sb, ctx.ps
    ones_bf = sb("ones_bf", [128, 1], BF16)
    ones_row = sb("ones_row", [1, 128])
    ones_rb = sb("ones_rb", [1, 128], BF16)
    ident = sb("ident", [128, 128])
    gate_sb = sb("gate_sb", [128, NCH * NE], BF16)
    pos_row = sb("pos_row", [1, TLOC])
    eps1 = sb("eps1", [1, 1])
    epsr = sb("epsr", [1, 1])
    row_mu = sb("row_mu", [1, TLOC])
    row_e2 = sb("row_e2", [1, TLOC])
    row_r = sb("row_r", [1, TLOC])
    mu_b = sb("mu_b", [128, TLOC])
    r_b = sb("r_b", [128, TLOC])
    scr = sb("scr", [128, TLOC])
    scr2 = sb("scr2", [128, TLOC])
    lg_tm = sb("lg_tm", [128, NE])
    m_tm = sb("m_tm", [128, NE])
    mt_tm = sb("mt_tm", [128, NE])
    l1_c = sb("l1_c", [128, 1])
    l2_c = sb("l2_c", [128, 1])
    rw_c = sb("rw_c", [128, 1])
    cnts_sb = sb("cnts_sb", [8, 2 * NE])
    selp = sb("selp", [8, 1])
    ones8c = sb("ones8c", [8, 1])
    off0_c = sb("off0_c", [NE, 1])
    off1_c = sb("off1_c", [NE, 1])
    rrow = sb("rrow", [1, 128])
    sc1 = sb("sc1", [1, 1])
    sc2 = sb("sc2", [1, 1])
    iot = sb("iot", [1, 64])
    om_row = sb("om_row", [1, 128])
    ident_b = sb("ident_b", [128, 128], BF16)

    def guard(name, shape, dt=F32):
        return nc.sbuf_tensor(name, shape, dt)

    pA = ps("pA", [128, 512])
    pB = ps("pB", [128, 512])
    pC = ps("pC", [128, TLOC])
    pD = ps("pD", [128, TLOC])
    pT = ps("pT", [128, 128], BF16)

    w = dict(pC=pC, pD=pD, ones_bf=ones_bf, ones_row=ones_row,
             row_mu=row_mu, row_e2=row_e2, row_r=row_r, mu_b=mu_b, r_b=r_b,
             scr=scr, eps1=eps1)

    with nc.Block() as blk:
        ctx.blk = blk

        @blk.sync
        def _(sy):
            ctx.dma("sl", sy.dma_start(out=ident[:], in_=ident_in))
            ctx.dma("sl", sy.dma_start(out=pos_row[:], in_=pos_in))
            ctx.dma("sl", sy.dma_start(out=cnts_sb[:], in_=cnts_in))
            ctx.dma("sl", sy.dma_start(out=selp[:], in_=selp_in))

        @blk.gpsimd
        def _(g):
            ctx.dma("gl", g.dma_start(out=gate_sb.ap().rearrange("p (c e) -> p c e", e=NE),
                                      in_=gate_wT.rearrange("(c p) e -> p c e", p=128)))
            ctx.inc("cg", g.iota(iot[:, :], [[1, 64]], base=0, channel_multiplier=0,
                                 allow_small_or_imprecise_dtypes=True))

        @blk.vector
        def _(v):
            ctx.inc("cv", v.memset(ones_bf[:], 1.0))
            ctx.chain(v, "cv", v.memset(ones_row[:], 1.0))
            ctx.chain(v, "cv", v.memset(ones_rb[:], 1.0))
            ctx.chain(v, "cv", v.memset(eps1[:, :], LN_EPS))
            ctx.chain(v, "cv", v.memset(epsr[:, :], RMS_EPS))
            ctx.chain(v, "cv", v.memset(ones8c[:, :], 1.0))

        @blk.vector
        def _(v):
            ctx.wait(v, "sl")
            ctx.chain(v, "cv", v.tensor_copy(out=ident_b[:, :], in_=ident[:, :]))

        @blk.tensor
        def _(t):
            ctx.wait(t, "sl", "cv")
            ctx.inc("ct", t.matmul(pA[0:1, 0:NE], selp[0:8, 0:1],
                                   cnts_sb[0:8, 0:NE], start=True, stop=True))
            ctx.inc("ct", t.matmul(pB[0:1, 0:NE], selp[0:8, 0:1],
                                   cnts_sb[0:8, NE:2 * NE], start=True, stop=True))
            ctx.inc("ct", t.matmul(pC[0:1, 0:NE], ones8c[0:8, 0:1],
                                   cnts_sb[0:8, 0:NE], start=True, stop=True))

        @blk.scalar
        def _(s):
            ctx.wait(s, "ct")
            ctx.chain(s, "cs", s.copy(rrow[0:1, 0:NE], pA[0:1, 0:NE]))
            ctx.chain(s, "cs", s.copy(rrow[0:1, NE:2 * NE], pB[0:1, 0:NE]))
            ctx.chain(s, "cs", s.copy(rrow[0:1, 2 * NE:3 * NE], pC[0:1, 0:NE]))

        @blk.vector
        def _(v):
            ctx.wait(v, "cs")
            ctx.chain(v, "cv", v.tensor_add(rrow[0:1, NE:2 * NE], rrow[0:1, NE:2 * NE],
                                            rrow[0:1, 2 * NE:3 * NE]))

        for oi, dst in [(0, off0_c), (1, off1_c)]:
            @blk.tensor
            def _(t, oi=oi):
                ctx.wait(t, "cv", "cs")
                ctx.inc("ct", t.matmul(pB[0:NE, 0:1], rrow[0:1, oi * NE:(oi + 1) * NE],
                                       ones_row[0:1, 0:1], start=True, stop=True))

            @blk.scalar
            def _(s, dst=dst):
                ctx.wait(s, "ct")
                ctx.chain(s, "cs", s.copy(dst[:, 0:1], pB[0:NE, 0:1]))

        es_r1 = ExitStack()
        xln = es_r1.enter_context(guard("xln", [128, NCH * TLOC], BF16))
        es_r2 = ExitStack()
        x2_bf = es_r2.enter_context(guard("x2_bf", [128, NCH * TLOC], BF16))
        sqb = es_r2.enter_context(guard("sqb", [128, NCH * TLOC], BF16))
        w_own = es_r2.enter_context(guard("w_own", [NE, TLOC]))
        m0_l = es_r2.enter_context(guard("m0_l", [NE, TLOC]))
        m1_l = es_r2.enter_context(guard("m1_l", [NE, TLOC]))
        rk_l = es_r2.enter_context(guard("rk_l", [NE, TLOC]))
        rw1_row = es_r2.enter_context(guard("rw1_row", [1, TLOC]))
        rwch = es_r2.enter_context(guard("rwch", [1, 512]))
        em_scr = es_r2.enter_context(guard("em_scr", [NE, 512]))
        em_scr2 = es_r2.enter_context(guard("em_scr2", [NE, 512]))
        w["sq"] = sqb

        @blk.gpsimd
        def _(g):
            ctx.wait(g, "cv", "cs", "ct")
            ctx.dma("gl", g.dma_start(out=x2_bf.ap().rearrange("p (c t) -> p c t", t=TLOC),
                                      in_=x2b_in.rearrange("(c p) t -> p c t", p=128)))

        emit_ln(ctx, lambda ci: x2_bf[:, ci * TLOC:(ci + 1) * TLOC],
                lambda ci: xln[:, ci * TLOC:(ci + 1) * TLOC], w)

        for tau in range(8):
            off = tau * 128

            @blk.tensor
            def _(t, tau=tau):
                ctx.wait(t, "cv", "cs")
                for ci in range(NCH):
                    ctx.inc("ct", t.matmul(
                        pA[:, 0:NE],
                        xln[:, ci * TLOC + tau * 128: ci * TLOC + (tau + 1) * 128],
                        gate_sb[:, ci * NE:(ci + 1) * NE],
                        start=(ci == 0), stop=(ci == NCH - 1)))

            @blk.scalar
            def _(s):
                ctx.wait(s, "ct")
                ctx.chain(s, "cs", s.copy(lg_tm[:, :], pA[:, 0:NE]))

            @blk.vector
            def _(v):
                ctx.wait(v, "cs")
                ctx.chain(v, "cv", v.reduce_max(l1_c[:, :], lg_tm[:, :], axis=AX.X))
                ctx.chain(v, "cv", v.tensor_scalar(out=m_tm[:, :], in0=lg_tm[:, :],
                                                   scalar1=l1_c[:, 0:1], scalar2=None,
                                                   op0=ALU.is_equal))
                ctx.chain(v, "cv", v.scalar_tensor_tensor(out=mt_tm[:, :], in0=m_tm[:, :],
                                                          scalar=NEG, in1=lg_tm[:, :],
                                                          op0=ALU.mult, op1=ALU.add))
                ctx.chain(v, "cv", v.reduce_max(l2_c[:, :], mt_tm[:, :], axis=AX.X))
                ctx.chain(v, "cv", v.tensor_scalar(out=mt_tm[:, :], in0=lg_tm[:, :],
                                                   scalar1=l2_c[:, 0:1], scalar2=None,
                                                   op0=ALU.is_equal))
                ctx.chain(v, "cv", v.tensor_sub(rw_c[:, :], l1_c[:, :], l2_c[:, :]))

            @blk.scalar
            def _(s):
                ctx.wait(s, "cv")
                ctx.chain(s, "cs", s.activation(out=rw_c[:, :], in_=rw_c[:, :], func=AF.Sigmoid))

            @blk.tensor
            def _(t):
                ctx.wait(t, "cs")
                ctx.inc("ct", t.transpose(pB[0:NE, 0:128], m_tm[:, 0:NE], ident[:]))
                ctx.chain(t, "ct", t.matmul(pA[0:1, 0:128], rw_c[:, 0:1], ident[:],
                                            start=True, stop=True))

            @blk.scalar
            def _(s, off=off):
                ctx.wait(s, "ct")
                ctx.chain(s, "cs", s.copy(m0_l[:, off:off + 128], pB[0:NE, 0:128]))
                ctx.chain(s, "cs", s.copy(rw1_row[0:1, off:off + 128], pA[0:1, 0:128]))

            @blk.tensor
            def _(t):
                ctx.wait(t, "cs")
                ctx.inc("ct", t.transpose(pB[0:NE, 0:128], mt_tm[:, 0:NE], ident[:]))

            @blk.scalar
            def _(s, off=off):
                ctx.wait(s, "ct")
                ctx.chain(s, "cs", s.copy(m1_l[:, off:off + 128], pB[0:NE, 0:128]))

        @blk.vector
        def _(v):
            ctx.wait(v, "cs")
            ctx.chain(v, "cv", v.tensor_tensor_scan(out=rk_l[:, :], data0=m0_l[:, :],
                                                    data1=m0_l[:, :], initial=0.0,
                                                    op0=ALU.add, op1=ALU.bypass))
            ctx.chain(v, "cv", v.tensor_sub(rk_l[:, :], rk_l[:, :], m0_l[:, :]))
            ctx.chain(v, "cv", v.tensor_scalar(out=rk_l[:, :], in0=rk_l[:, :],
                                               scalar1=off0_c[:, 0:1], scalar2=float(CAP) - 0.5,
                                               op0=ALU.add, op1=ALU.is_le))
            ctx.chain(v, "cv", v.tensor_mul(m0_l[:, :], m0_l[:, :], rk_l[:, :]))
            ctx.chain(v, "cv", v.tensor_tensor_scan(out=rk_l[:, :], data0=m1_l[:, :],
                                                    data1=m1_l[:, :], initial=0.0,
                                                    op0=ALU.add, op1=ALU.bypass))
            ctx.chain(v, "cv", v.tensor_sub(rk_l[:, :], rk_l[:, :], m1_l[:, :]))
            ctx.chain(v, "cv", v.tensor_scalar(out=rk_l[:, :], in0=rk_l[:, :],
                                               scalar1=off1_c[:, 0:1], scalar2=float(CAP) - 0.5,
                                               op0=ALU.add, op1=ALU.is_le))
            ctx.chain(v, "cv", v.tensor_mul(m1_l[:, :], m1_l[:, :], rk_l[:, :]))

        for nn in range(2):
            o = nn * 512

            @blk.vector
            def _(v, o=o):
                ctx.wait(v, "cs")
                ctx.chain(v, "cv", v.tensor_scalar(out=rwch[0:1, :], in0=rw1_row[0:1, o:o + 512],
                                                   scalar1=-1.0, scalar2=1.0,
                                                   op0=ALU.mult, op1=ALU.add))

            @blk.tensor
            def _(t, o=o):
                ctx.wait(t, "cv")
                ctx.inc("ct", t.matmul(pA[0:NE, :], ones_row[0:1, 0:NE],
                                       rw1_row[0:1, o:o + 512], start=True, stop=True))
                ctx.inc("ct", t.matmul(pB[0:NE, :], ones_row[0:1, 0:NE],
                                       rwch[0:1, :], start=True, stop=True))

            @blk.vector
            def _(v, o=o):
                ctx.wait(v, "ct")
                ctx.chain(v, "cv", v.tensor_mul(em_scr[:, :], m0_l[:, o:o + 512], pA[0:NE, :]))
                ctx.chain(v, "cv", v.tensor_mul(em_scr2[:, :], m1_l[:, o:o + 512], pB[0:NE, :]))
                ctx.chain(v, "cv", v.tensor_add(w_own[:, o:o + 512], em_scr[:, :], em_scr2[:, :]))

        @blk.sync
        def _(sy):
            ctx.wait(sy, "cv")
            ctx.dma("sl", sy.dma_start(out=w_rows_d.ap(), in_=w_own[:, :]))

        @blk.vector
        def _(v):
            ctx.wait(v, "cs", "ct")

        es_r2.close()

        es_moe = ExitStack()
        x3_bf = es_moe.enter_context(guard("x3_bf", [128, NCH * TLOC], BF16))
        es_moe2 = ExitStack()
        hprime = es_moe2.enter_context(guard("hprime", [128, NE * 2 * TLOC], BF16))
        we_sb = es_moe2.enter_context(guard("we_sb", [128, 2 * NCH * FEXP], BF16))
        wb_e = es_moe2.enter_context(guard("wb_e", [128, TLOC]))
        w_row_sb = es_moe2.enter_context(guard("w_row_sb", [1, TLOC]))
        sg_sb = es_moe2.enter_context(guard("sg_sb", [128, TLOC]))
        su_sb = es_moe2.enter_context(guard("su_sb", [128, TLOC]))

        WE_SLOT = 2 * NCH * FEXP
        for e in range(NE):
            @blk.gpsimd
            def _(g, e=e):
                ctx.wait(g, "ct", "sl", "cv", "cs")
                ctx.dma("gl", g.dma_start(
                    out=we_sb.ap()[:, 0:NCH * FEXP].rearrange("p (c f) -> p c f", f=FEXP),
                    in_=w1T[e].rearrange("(c p) f -> p c f", p=128)))
                ctx.dma("gl", g.dma_start(
                    out=we_sb.ap()[:, NCH * FEXP:WE_SLOT].rearrange("p (c f) -> p c f", f=FEXP),
                    in_=w3T[e].rearrange("(c p) f -> p c f", p=128)))
                ctx.dma("gl", g.dma_start(out=w_row_sb.ap(), in_=w_rows_d.ap()[e:e + 1, :]))

            @blk.tensor
            def _(t):
                ctx.wait(t, "gl", "cs", "cv")
                for nn in range(2):
                    ctx.inc("ct", t.matmul(pC[:, nn * 512:(nn + 1) * 512], ones_row[0:1, :],
                                           w_row_sb[0:1, nn * 512:(nn + 1) * 512],
                                           start=True, stop=True))

            @blk.scalar
            def _(s):
                ctx.wait(s, "ct")
                ctx.chain(s, "cs", s.copy(wb_e[:, :], pC[:, 0:TLOC]))

            for m2 in range(2):
                @blk.tensor
                def _(t, m2=m2):
                    ctx.wait(t, "cs", "cv")
                    for nn in range(2):
                        pdst = pA if nn == 0 else pB
                        for ci in range(NCH):
                            ctx.inc("ct", t.matmul(
                                pdst[:, 0:512],
                                we_sb[:, ci * FEXP + m2 * 128: ci * FEXP + m2 * 128 + 128],
                                xln[:, ci * TLOC + nn * 512: ci * TLOC + (nn + 1) * 512],
                                start=(ci == 0), stop=(ci == NCH - 1)))
                    for nn in range(2):
                        for ci in range(NCH):
                            ctx.inc("ct", t.matmul(
                                pD[:, nn * 512:(nn + 1) * 512],
                                we_sb[:, NCH * FEXP + ci * FEXP + m2 * 128:
                                      NCH * FEXP + ci * FEXP + m2 * 128 + 128],
                                xln[:, ci * TLOC + nn * 512: ci * TLOC + (nn + 1) * 512],
                                start=(ci == 0), stop=(ci == NCH - 1)))

                @blk.scalar
                def _(s):
                    ctx.wait(s, "ct")
                    ctx.chain(s, "cs", s.copy(sg_sb[:, 0:512], pA[:, 0:512]))
                    ctx.chain(s, "cs", s.copy(sg_sb[:, 512:TLOC], pB[:, 0:512]))
                    ctx.chain(s, "cs", s.copy(su_sb[:, :], pD[:, 0:TLOC]))

                @blk.vector
                def _(v):
                    ctx.wait(v, "cs")
                    ctx.chain(v, "cv", v.tensor_mul(scr[:, :], sg_sb[:, :], su_sb[:, :]))

                @blk.scalar
                def _(s):
                    ctx.wait(s, "cv")
                    ctx.chain(s, "cs", s.activation(out=scr2[:, :], in_=scr[:, :], func=AF.Silu))

                @blk.vector
                def _(v, e=e, m2=m2):
                    ctx.wait(v, "cs")
                    ctx.chain(v, "cv", v.tensor_mul(
                        hprime[:, (e * 2 + m2) * TLOC:(e * 2 + m2 + 1) * TLOC],
                        scr2[:, :], wb_e[:, :]))

        es_w2 = ExitStack()
        w2_sb = es_w2.enter_context(guard("w2_sb", [128, NE * 2 * 128], BF16))
        for mo in range(NCH):
            @blk.gpsimd
            def _(g, mo=mo):
                ctx.wait(g, "ct", "cv", "cs")
                for e in range(NE):
                    ctx.dma("gl", g.dma_start(
                        out=w2_sb.ap()[:, e * 256: e * 256 + 256]
                            .rearrange("p (k m) -> p k m", m=128),
                        in_=w2T[e].rearrange("(k p) m -> p k m", p=128)[:, :, mo * 128:(mo + 1) * 128]))

            @blk.sync
            def _(sy, mo=mo):
                ctx.wait(sy, "cv")
                ctx.dma("sl", sy.dma_start(
                    out=scr2[:, 0:TLOC],
                    in_=x2_in.rearrange("(c p) t -> c p t", p=128)[mo]))

            @blk.tensor
            def _(t, mo=mo):
                ctx.wait(t, "gl", "cv")
                for nn in range(2):
                    pdst = pA if nn == 0 else pB
                    first = True
                    for e in range(NE):
                        for k2 in range(2):
                            ctx.inc("ct", t.matmul(
                                pdst[:, 0:512],
                                w2_sb[:, e * 256 + k2 * 128: e * 256 + k2 * 128 + 128],
                                hprime[:, (e * 2 + k2) * TLOC + nn * 512:
                                       (e * 2 + k2) * TLOC + nn * 512 + 512],
                                start=first, stop=(e == NE - 1 and k2 == 1)))
                            first = False

            @blk.vector
            def _(v, mo=mo):
                ctx.wait(v, "ct", "sl", "so")
                for nn in range(2):
                    pdst = pA if nn == 0 else pB
                    ctx.chain(v, "cv", v.scalar_tensor_tensor(
                        out=scr[:, nn * 512:(nn + 1) * 512],
                        in0=pdst[:, 0:512], scalar=0.5,
                        in1=scr2[:, nn * 512:(nn + 1) * 512],
                        op0=ALU.mult, op1=ALU.add))
                ctx.chain(v, "cv", v.tensor_copy(out=x3_bf[:, mo * TLOC:(mo + 1) * TLOC],
                                                 in_=scr[:, :]))

            @blk.sync
            def _(sy, mo=mo):
                ctx.wait(sy, "cv")
                ctx.dma("so", sy.dma_start(
                    out=x3_d.ap().rearrange("(c p) t -> c p t", p=128)[mo], in_=scr[:, :]))

        if debug:
            @blk.sync
            def _(sy):
                ctx.wait(sy, "so", "cv")
                ctx.dma("so", sy.dma_start(out=dbg_x3, in_=x3_d.ap()))
                ctx.dma("so", sy.dma_start(out=dbg_w, in_=w_rows_d.ap()))

        es_w2.close()
        es_moe2.close()

        es_l3 = ExitStack()
        sqb3 = es_l3.enter_context(guard("sqb3", [128, NCH * TLOC], BF16))
        w["sq"] = sqb3
        emit_ln(ctx, lambda ci: x3_bf[:, ci * TLOC:(ci + 1) * TLOC],
                lambda ci: xln[:, ci * TLOC:(ci + 1) * TLOC], w)
        es_l3.close()
        es_moe.close()

        x_acc = ctx.nc.alloc_sbuf_tensor("x_acc", [128, NCH * TLOC], F32, side="right")
        es_f = ExitStack()
        h_i = es_f.enter_context(guard("h_i", [128, NCH * TLOC], BF16))
        fcol = es_f.enter_context(guard("fcol", [128, 2 * NCH * 128], BF16))

        for ic in range(4):
            for mo2 in range(NCH):
                slot = mo2 % 2
                cb = slot * NCH * 128

                @blk.gpsimd
                def _(g, ic=ic, mo2=mo2, cb=cb):
                    ctx.wait(g, "ct", "cv", "cs")
                    ctx.dma("gl", g.dma_start(
                        out=fcol.ap()[:, cb:cb + NCH * 128].rearrange("p (c m) -> p c m", m=128),
                        in_=wguT.rearrange("(c p) m -> p c m", p=128)
                            [:, :, ic * 2048 + mo2 * 128: ic * 2048 + (mo2 + 1) * 128]))

                @blk.tensor
                def _(t, cb=cb):
                    ctx.wait(t, "gl", "cs", "cv")
                    for nn in range(2):
                        pdst = pA if nn == 0 else pB
                        for ci in range(NCH):
                            ctx.inc("ct", t.matmul(
                                pdst[:, 0:512], fcol[:, cb + ci * 128:cb + (ci + 1) * 128],
                                xln[:, ci * TLOC + nn * 512: ci * TLOC + (nn + 1) * 512],
                                start=(ci == 0), stop=(ci == NCH - 1)))

                @blk.gpsimd
                def _(g, ic=ic, mo2=mo2, cb=cb):
                    ctx.wait(g, "ct")
                    ctx.dma("gl", g.dma_start(
                        out=fcol.ap()[:, cb:cb + NCH * 128].rearrange("p (c m) -> p c m", m=128),
                        in_=wguT.rearrange("(c p) m -> p c m", p=128)
                            [:, :, INTER + ic * 2048 + mo2 * 128: INTER + ic * 2048 + (mo2 + 1) * 128]))

                @blk.scalar
                def _(s):
                    ctx.wait(s, "ct")
                    ctx.chain(s, "cs", s.activation(out=scr[:, 0:512], in_=pA[:, 0:512], func=AF.Silu))
                    ctx.chain(s, "cs", s.activation(out=scr[:, 512:TLOC], in_=pB[:, 0:512], func=AF.Silu))

                @blk.tensor
                def _(t, cb=cb):
                    ctx.wait(t, "gl", "cs")
                    for nn in range(2):
                        pdst = pA if nn == 0 else pB
                        for ci in range(NCH):
                            ctx.inc("ct", t.matmul(
                                pdst[:, 0:512], fcol[:, cb + ci * 128:cb + (ci + 1) * 128],
                                xln[:, ci * TLOC + nn * 512: ci * TLOC + (nn + 1) * 512],
                                start=(ci == 0), stop=(ci == NCH - 1)))

                @blk.scalar
                def _(s):
                    ctx.wait(s, "ct")
                    ctx.chain(s, "cs", s.copy(scr2[:, 0:512], pA[:, 0:512]))
                    ctx.chain(s, "cs", s.copy(scr2[:, 512:TLOC], pB[:, 0:512]))

                @blk.vector
                def _(v, mo2=mo2):
                    ctx.wait(v, "cs")
                    ctx.chain(v, "cv", v.tensor_mul(h_i[:, mo2 * TLOC:(mo2 + 1) * TLOC],
                                                    scr[:, :], scr2[:, :]))

            for mo in range(NCH):
                slot = mo % 2
                cb = slot * NCH * 128

                @blk.gpsimd
                def _(g, ic=ic, mo=mo, cb=cb):
                    ctx.wait(g, "ct", "cv", "cs")
                    ctx.dma("gl", g.dma_start(
                        out=fcol.ap()[:, cb:cb + NCH * 128].rearrange("p (c m) -> p c m", m=128),
                        in_=wdT.rearrange("(c p) m -> p c m", p=128)
                            [:, ic * NCH:(ic + 1) * NCH, mo * 128:(mo + 1) * 128]))

                @blk.tensor
                def _(t, ic=ic, mo=mo, cb=cb):
                    ctx.wait(t, "gl", "cv", "cs")
                    for nn in range(2):
                        pdst = pA if nn == 0 else pB
                        for ci in range(NCH):
                            ctx.inc("ct", t.matmul(
                                pdst[:, 0:512], fcol[:, cb + ci * 128:cb + (ci + 1) * 128],
                                h_i[:, ci * TLOC + nn * 512: ci * TLOC + (nn + 1) * 512],
                                start=(ci == 0), stop=(ci == NCH - 1)))

                if ic == 0:
                    @blk.sync
                    def _(sy, mo=mo):
                        ctx.wait(sy, "cv")
                        ctx.dma("sl", sy.dma_start(
                            out=scr2[:, 0:TLOC],
                            in_=x3_d.ap().rearrange("(c p) t -> c p t", p=128)[mo]))

                    @blk.vector
                    def _(v, mo=mo):
                        ctx.wait(v, "ct", "sl")
                        for nn in range(2):
                            pdst = pA if nn == 0 else pB
                            ctx.chain(v, "cv", v.scalar_tensor_tensor(
                                out=x_acc[:, mo * TLOC + nn * 512: mo * TLOC + (nn + 1) * 512],
                                in0=pdst[:, 0:512], scalar=0.5,
                                in1=scr2[:, nn * 512:(nn + 1) * 512],
                                op0=ALU.mult, op1=ALU.add))
                else:
                    @blk.vector
                    def _(v, mo=mo):
                        ctx.wait(v, "ct")
                        for nn in range(2):
                            pdst = pA if nn == 0 else pB
                            ctx.chain(v, "cv", v.scalar_tensor_tensor(
                                out=x_acc[:, mo * TLOC + nn * 512: mo * TLOC + (nn + 1) * 512],
                                in0=pdst[:, 0:512], scalar=0.5,
                                in1=x_acc[:, mo * TLOC + nn * 512: mo * TLOC + (nn + 1) * 512],
                                op0=ALU.mult, op1=ALU.add))

        es_f.close()
        es_r1.close()

        @blk.sync
        def _(sy):
            ctx.wait(sy, "cv")
            ctx.dma("so", sy.dma_start(out=x4_out.rearrange("(c p) t -> p c t", p=128),
                                       in_=x_acc.ap().rearrange("p (c t) -> p c t", t=TLOC)))

        es_q = ExitStack()
        scr_i = es_q.enter_context(guard("scr_i", [128, TLOC], I32))
        cc2 = es_q.enter_context(guard("cc2", [128, TLOC]))
        ss2 = es_q.enter_context(guard("ss2", [128, TLOC]))
        gcol_all = es_q.enter_context(guard("gcol_all", [128, 3 * NCH]))
        ag3 = es_q.enter_context(guard("ag3", [128, 3 * 8]))
        xq = es_q.enter_context(guard("xq", [128, NCH * 512], BF16))
        arow = es_q.enter_context(guard("arow", [1, TLOC]))
        ainvrow = es_q.enter_context(guard("ainvrow", [1, TLOC]))
        sc_b = es_q.enter_context(guard("sc_b", [128, TLOC]))
        a_b = es_q.enter_context(guard("a_b", [128, TLOC]))
        rstd_b = es_q.enter_context(guard("rstd_b", [128, TLOC]))
        acol = es_q.enter_context(guard("acol", [128, 8]))
        grow_sb = es_q.enter_context(guard("grow_sb", [1, D]))
        ws3 = es_q.enter_context(guard("ws3", [1, 8]))
        wsc = es_q.enter_context(guard("wsc", [128, 1]))
        es_q1 = ExitStack()
        qsq = es_q1.enter_context(guard("qsq", [128, TLOC], BF16))
        gb_all = es_q1.enter_context(guard("gb_all", [128, 3 * NCH * 128]))
        xr_tm = es_q1.enter_context(guard("xr_tm", [128, 2048]))
        scr_tm = es_q1.enter_context(guard("scr_tm", [128, 2048]))

        @blk.scalar
        def _(s):
            ctx.wait(s, "cg")
            ctx.chain(s, "cs", s.activation(out=iot[:, :], in_=iot[:, :], func=AF.Exp,
                                            scale=-LN1E4 / 64.0))
            for rep in range(2):
                ctx.chain(s, "cs", s.copy(om_row[0:1, rep * 64:(rep + 1) * 64], iot[0:1, :]))

        @blk.tensor
        def _(t):
            ctx.wait(t, "cs", "sl")
            for nn in range(2):
                ctx.inc("ct", t.matmul(pC[:, nn * 512:(nn + 1) * 512], om_row[0:1, :],
                                       pos_row[0:1, nn * 512:(nn + 1) * 512],
                                       start=True, stop=True))

        emit_sin(ctx, pC, ss2, scr, scr2, scr_i, 0.0, TLOC)
        emit_sin(ctx, pC, cc2, scr, scr2, scr_i, HALF_PI, TLOC)

        for gi, grow in enumerate([gq_r, gk_r, gv_r]):
            @blk.gpsimd
            def _(g, gi=gi, grow=grow):
                ctx.wait(g, "cv", "cs", "ct")
                ctx.dma("gl", g.dma_start(out=grow_sb.ap(), in_=grow))

            @blk.tensor
            def _(t, gi=gi):
                ctx.wait(t, "gl")
                for q4 in range(4):
                    ctx.inc("ct", t.matmul(pC[:, q4 * 512:(q4 + 1) * 512] if q4 < 2 else
                                           pD[:, (q4 - 2) * 512:(q4 - 1) * 512], ones_row[0:1, :],
                                           grow_sb[0:1, q4 * 512:(q4 + 1) * 512], start=True, stop=True))

            @blk.scalar
            def _(s, gi=gi):
                ctx.wait(s, "ct")
                ctx.chain(s, "cs", s.copy(gb_all[:, gi * NCH * 128: gi * NCH * 128 + TLOC],
                                          pC[:, 0:TLOC]))
                ctx.chain(s, "cs", s.copy(gb_all[:, gi * NCH * 128 + TLOC:(gi + 1) * NCH * 128],
                                          pD[:, 0:TLOC]))

            for ci in range(NCH):
                @blk.tensor
                def _(t, gi=gi, ci=ci):
                    ctx.wait(t, "cs")
                    ctx.inc("ct", t.transpose(pB[:, 0:128], gb_all[:, gi * NCH * 128 + ci * 128:
                                                               gi * NCH * 128 + (ci + 1) * 128],
                                              ident[:]))

                @blk.scalar
                def _(s, gi=gi, ci=ci):
                    ctx.wait(s, "ct")
                    ctx.chain(s, "cs", s.copy(gcol_all[:, gi * NCH + ci:gi * NCH + ci + 1],
                                              pB[:, 0:1]))

        for ci in range(NCH):
            @blk.scalar
            def _(s, ci=ci):
                ctx.wait(s, "cv")
                if ci > 0:
                    ctx.wait(s, "ct")
                ctx.chain(s, "cs", s.activation(out=qsq[:, :],
                                                in_=x_acc[:, ci * TLOC:(ci + 1) * TLOC],
                                                func=AF.Square))

            @blk.tensor
            def _(t, ci=ci):
                ctx.wait(t, "cs")
                for nn in range(2):
                    ctx.inc("ct", t.matmul(pD[0:1, nn * 512:(nn + 1) * 512], ones_bf[:, 0:1],
                                           qsq[:, nn * 512:(nn + 1) * 512],
                                           start=(ci == 0), stop=(ci == NCH - 1),
                                           skip_group_check=True))

        @blk.scalar
        def _(s):
            ctx.wait(s, "ct")
            ctx.chain(s, "cs", s.activation(out=row_r[:, :], in_=pD[0:1, 0:TLOC], func=AF.Sqrt,
                                            scale=1.0 / D, bias=epsr[0:1, 0:1]))

        @blk.vector
        def _(v):
            ctx.wait(v, "cs")
            ctx.chain(v, "cv", v.reciprocal(row_r[:, :], row_r[:, :]))

        @blk.tensor
        def _(t):
            ctx.wait(t, "cv")
            for nn in range(2):
                ctx.inc("ct", t.matmul(pC[:, nn * 512:(nn + 1) * 512], ones_row[0:1, :],
                                       row_r[0:1, nn * 512:(nn + 1) * 512], start=True, stop=True))

        @blk.scalar
        def _(s):
            ctx.wait(s, "ct")
            ctx.chain(s, "cs", s.copy(rstd_b[:, :], pC[:, 0:TLOC]))

        for tau in range(8):
            for ci in range(NCH):
                @blk.vector
                def _(v, tau=tau, ci=ci):
                    ctx.wait(v, "cs", "ct")
                    ctx.chain(v, "cv", v.tensor_mul(
                        scr[:, 0:128],
                        x_acc[:, ci * TLOC + tau * 128: ci * TLOC + (tau + 1) * 128],
                        rstd_b[:, tau * 128:(tau + 1) * 128]))

                @blk.tensor
                def _(t, ci=ci):
                    ctx.wait(t, "cv")
                    ctx.inc("ct", t.transpose(pB[:, 0:128], scr[:, 0:128], ident[:]))

                @blk.scalar
                def _(s, ci=ci):
                    ctx.wait(s, "ct")
                    ctx.chain(s, "cs", s.copy(xr_tm[:, ci * 128:(ci + 1) * 128], pB[:, 0:128]))

            for gi in range(3):
                @blk.vector
                def _(v, tau=tau, gi=gi):
                    ctx.wait(v, "cs")
                    ctx.chain(v, "cv", v.tensor_mul(scr_tm[:, :], xr_tm[:, :],
                                                    gb_all[:, gi * 2048:(gi + 1) * 2048]))
                    ctx.chain(v, "cv", v.reduce_max(ag3[:, gi * 8 + tau:gi * 8 + tau + 1],
                                                    scr_tm[:, :], axis=AX.X,
                                                    apply_absolute_value=True))

        es_q1.close()
        es_q2 = ExitStack()
        tern = es_q2.enter_context(guard("tern", [128, NCH * 128], BF16))
        tpos = es_q2.enter_context(guard("tpos", [128, NCH * 128], BF16))
        wcol = es_q2.enter_context(guard("wcol", [128, NCH * 128], BF16))
        wsbuf = es_q2.enter_context(guard("wsbuf", [128, 4096], BF16))
        z_v = es_q2.enter_context(guard("z_v", [128, 512]))
        u_v = es_q2.enter_context(guard("u_v", [128, 512]))
        b_v = es_q2.enter_context(guard("b_v", [128, 512]))
        z_g = es_q2.enter_context(guard("z_g", [128, 512]))
        u_g = es_q2.enter_context(guard("u_g", [128, 512]))
        b_g = es_q2.enter_context(guard("b_g", [128, 512]))
        pchunk = es_q2.enter_context(guard("pchunk", [128, TLOC]))
        t1ch = es_q2.enter_context(guard("t1ch", [128, TLOC]))
        och = es_q2.enter_context(guard("och", [128, TLOC], BF16))

        @blk.vector
        def _(v):
            ctx.wait(v, "cs", "ct")
            ctx.chain(v, "cv", v.tensor_scalar(out=ag3[:, :], in0=ag3[:, :],
                                               scalar1=1e-5, scalar2=None, op0=ALU.max))

        STAIR = [128.0, 64.0, 32.0, 16.0, 8.0, 4.0, 2.0, 1.0]

        for gi, (wdram, odram, nmo, kind) in enumerate([
                (wqT, q_out, NCH, "ropeq"), (wkT, k_out, 4, "ropeq"), (wvT, v_out, 4, "copy")]):
            npiece = nmo * 2
            npiece = 8 if gi == 0 else 2

            @blk.vector
            def _(v):
                ctx.wait(v, "cs", "ct", "cg")
                ctx.chain(v, "cv", v.memset(wsc[:, :], 0.0))

            for pp in range(npiece):
                @blk.gpsimd
                def _(g, wdram=wdram, pp=pp, nmo=nmo):
                    ctx.wait(g, "cv", "ct", "cs")
                    ncols = nmo * 128 // npiece if False else (2048 // 8 if gi == 0 else 256)
                    ctx.dma("gl", g.dma_start(
                        out=wsbuf.ap().rearrange("p (c m) -> p c m", m=256),
                        in_=wdram.rearrange("(c p) m -> p c m", p=128)
                            [:, :, pp * 256:(pp + 1) * 256]))

                @blk.vector
                def _(v):
                    ctx.wait(v, "gl")
                    ctx.chain(v, "cv", v.tensor_reduce(mcol_ := wsc[:, 0:1], wsbuf[:, :],
                                                       axis=AX.X, op=ALU.add,
                                                       apply_absolute_value=True)
                              if False else
                              v.tensor_reduce(rrowp := l1_c[:, 0:1], wsbuf[:, :],
                                              axis=AX.X, op=ALU.add,
                                              apply_absolute_value=True))
                    ctx.chain(v, "cv", v.tensor_add(wsc[:, 0:1], wsc[:, 0:1], l1_c[:, 0:1]))

            @blk.tensor
            def _(t):
                ctx.wait(t, "cv")
                ctx.inc("ct", t.matmul(pA[0:1, 0:128], wsc[:, 0:1], ident[:],
                                       start=True, stop=True))

            @blk.scalar
            def _(s):
                ctx.wait(s, "ct")
                ctx.chain(s, "cs", s.copy(rrow[:, :], pA[0:1, 0:128]))

            @blk.vector
            def _(v, gi=gi, nmo=nmo):
                ctx.wait(v, "cs")
                ctx.chain(v, "cv", v.reduce_sum(ws3[0:1, gi:gi + 1], rrow[:, :], axis=AX.X))
                ctx.chain(v, "cv", v.tensor_scalar(out=ws3[0:1, gi:gi + 1],
                                                   in0=ws3[0:1, gi:gi + 1],
                                                   scalar1=1.0 / (D * nmo * 128), scalar2=1e-5,
                                                   op0=ALU.mult, op1=ALU.max))
                ctx.chain(v, "cv", v.reciprocal(ws3[0:1, gi + 4:gi + 5], ws3[0:1, gi:gi + 1]))

            @blk.tensor
            def _(t, gi=gi):
                ctx.wait(t, "cv")
                ctx.inc("ct", t.matmul(pB[:, 0:1], ones_row[0:1, :], ws3[0:1, gi + 4:gi + 5],
                                       start=True, stop=True))

            @blk.scalar
            def _(s):
                ctx.wait(s, "ct")
                ctx.chain(s, "cs", s.copy(wsc[:, 0:1], pB[:, 0:1]))

            for nn in range(2):
                hb = nn * 512

                @blk.vector
                def _(v, gi=gi):
                    ctx.wait(v, "cs", "ct")
                    pass

                for tau in range(4):
                    tg = nn * 4 + tau

                    @blk.tensor
                    def _(t, gi=gi, tg=tg):
                        ctx.wait(t, "cv", "cs")
                        ctx.inc("ct", t.matmul(pA[0:1, 0:128], ag3[:, gi * 8 + tg:gi * 8 + tg + 1],
                                               ident[:], start=True, stop=True))

                    @blk.scalar
                    def _(s, tau=tau, hb=hb):
                        ctx.wait(s, "ct")
                        ctx.chain(s, "cs", s.copy(arow[0:1, hb + tau * 128: hb + (tau + 1) * 128],
                                                  pA[0:1, 0:128]))

                @blk.vector
                def _(v, hb=hb, gi=gi):
                    ctx.wait(v, "cs")
                    ctx.chain(v, "cv", v.tensor_scalar(out=ainvrow[0:1, hb:hb + 512],
                                                       in0=arow[0:1, hb:hb + 512],
                                                       scalar1=1.0 / 127.0, scalar2=None,
                                                       op0=ALU.mult))
                    ctx.chain(v, "cv", v.reciprocal(arow[0:1, hb:hb + 512], arow[0:1, hb:hb + 512]))
                    ctx.chain(v, "cv", v.tensor_scalar(out=arow[0:1, hb:hb + 512],
                                                       in0=arow[0:1, hb:hb + 512],
                                                       scalar1=127.0, scalar2=None, op0=ALU.mult))
                    ctx.chain(v, "cv", v.tensor_scalar(out=ainvrow[0:1, hb:hb + 512],
                                                       in0=ainvrow[0:1, hb:hb + 512],
                                                       scalar1=ws3[0:1, gi:gi + 1], scalar2=None,
                                                       op0=ALU.mult))

                @blk.tensor
                def _(t, hb=hb):
                    ctx.wait(t, "cv")
                    ctx.inc("ct", t.matmul(pC[:, 0:512], ones_row[0:1, :],
                                           arow[0:1, hb:hb + 512], start=True, stop=True))
                    ctx.inc("ct", t.matmul(pD[:, 0:512], ones_row[0:1, :],
                                           ainvrow[0:1, hb:hb + 512], start=True, stop=True))

                @blk.scalar
                def _(s):
                    ctx.wait(s, "ct")
                    ctx.chain(s, "cs", s.copy(a_b[:, 0:512], pC[:, 0:512]))
                    ctx.chain(s, "cs", s.copy(sc_b[:, 0:512], pD[:, 0:512]))

                for ci in range(NCH):
                    if ci % 2 == 0:
                        eng_name, sem, zT, uT, bT = "vector", "cv", z_v, u_v, b_v
                    else:
                        eng_name, sem, zT, uT, bT = "gpsimd", "cg", z_g, u_g, b_g

                    def quant_body(E, ci=ci, gi=gi, hb=hb, sem=sem, zT=zT, uT=uT, bT=bT):
                        ctx.wait(E, "cs", "ct")
                        if sem == "cv":
                            ctx.wait(E, "cg")
                        else:
                            ctx.wait(E, "cv", "gl")
                        ctx.chain(E, sem, E.tensor_mul(zT[:, :],
                                                       x_acc[:, ci * TLOC + hb: ci * TLOC + hb + 512],
                                                       rstd_b[:, hb:hb + 512]))
                        ctx.chain(E, sem, E.tensor_scalar(out=zT[:, :], in0=zT[:, :],
                                                          scalar1=gcol_all[:, gi * NCH + ci:gi * NCH + ci + 1],
                                                          scalar2=None, op0=ALU.mult))
                        ctx.chain(E, sem, E.tensor_mul(zT[:, :], zT[:, :], a_b[:, 0:512]))
                        ctx.chain(E, sem, E.tensor_scalar(out=uT[:, :], in0=zT[:, :],
                                                          scalar1=128.5, scalar2=None, op0=ALU.add))
                        for wv_ in STAIR:
                            ctx.chain(E, sem, E.tensor_scalar(out=bT[:, :], in0=uT[:, :],
                                                              scalar1=wv_, scalar2=wv_,
                                                              op0=ALU.is_ge, op1=ALU.mult))
                            ctx.chain(E, sem, E.tensor_sub(uT[:, :], uT[:, :], bT[:, :]))
                        ctx.chain(E, sem, E.tensor_sub(zT[:, :], zT[:, :], uT[:, :]))
                        ctx.chain(E, sem, E.tensor_scalar(out=xq[:, ci * 512:(ci + 1) * 512],
                                                          in0=zT[:, :], scalar1=0.5, scalar2=None,
                                                          op0=ALU.add))

                    if eng_name == "vector":
                        @blk.vector
                        def _(v, qb=quant_body):
                            qb(v)
                    else:
                        @blk.gpsimd
                        def _(g, qb=quant_body):
                            qb(g)

                for mo in range(nmo):
                    @blk.gpsimd
                    def _(g, wdram=wdram, mo=mo):
                        ctx.wait(g, "ct", "cv", "cs")
                        ctx.dma("gl", g.dma_start(
                            out=wcol.ap().rearrange("p (c m) -> p c m", m=128),
                            in_=wdram.rearrange("(c p) m -> p c m", p=128)
                                [:, :, mo * 128:(mo + 1) * 128]))

                    @blk.vector
                    def _(v):
                        ctx.wait(v, "gl", "ct")
                        ctx.chain(v, "cv", v.tensor_scalar(out=tpos[:, :], in0=wcol[:, :],
                                                           scalar1=wsc[:, 0:1], scalar2=0.5,
                                                           op0=ALU.mult, op1=ALU.is_ge))
                        ctx.chain(v, "cv", v.tensor_scalar(out=tern[:, :], in0=wcol[:, :],
                                                           scalar1=wsc[:, 0:1], scalar2=-0.5,
                                                           op0=ALU.mult, op1=ALU.is_le))
                        ctx.chain(v, "cv", v.tensor_sub(tern[:, :], tpos[:, :], tern[:, :]))

                    @blk.tensor
                    def _(t, mo=mo):
                        ctx.wait(t, "cv", "cg", "cs")
                        for ci in range(NCH):
                            ctx.inc("ct", t.matmul(
                                pA[:, 0:512], tern[:, ci * 128:(ci + 1) * 128],
                                xq[:, ci * 512:(ci + 1) * 512],
                                start=(ci == 0), stop=(ci == NCH - 1)))

                    @blk.vector
                    def _(v, hb=hb):
                        ctx.wait(v, "ct", "so")
                        ctx.chain(v, "cv", v.tensor_mul(pchunk[:, hb:hb + 512], pA[:, 0:512],
                                                        sc_b[:, 0:512]))

                    if kind == "ropeq":
                        @blk.vector
                        def _(v, hb=hb):
                            ctx.chain(v, "cv", v.tensor_mul(t1ch[:, hb:hb + 512],
                                                            pchunk[:, hb:hb + 512],
                                                            cc2[:, hb:hb + 512]))
                            ctx.chain(v, "cv", v.tensor_mul(pC[:, hb:hb + 512],
                                                            pchunk[:, hb:hb + 512],
                                                            ss2[:, hb:hb + 512]))
                            ctx.chain(v, "cv", v.tensor_sub(och[0:64, hb:hb + 512],
                                                            t1ch[0:64, hb:hb + 512],
                                                            pC[64:128, hb:hb + 512]))
                            ctx.chain(v, "cv", v.tensor_add(och[64:128, hb:hb + 512],
                                                            pC[0:64, hb:hb + 512],
                                                            t1ch[64:128, hb:hb + 512]))
                    else:
                        @blk.vector
                        def _(v, hb=hb):
                            ctx.chain(v, "cv", v.tensor_copy(out=och[:, hb:hb + 512],
                                                             in_=pchunk[:, hb:hb + 512]))

                    @blk.sync
                    def _(sy, odram=odram, mo=mo, hb=hb):
                        ctx.wait(sy, "cv")
                        ctx.dma("so", sy.dma_start(out=odram[mo * 128:(mo + 1) * 128, hb:hb + 512],
                                                   in_=och[:, hb:hb + 512]))

        @blk.sync
        def _(sy):
            ctx.wait(sy, "so")

        es_q2.close()
        es_q.close()

        ctx.es.close()
        return ctx



def build_launch_D():
    ctx = Ctx("D")
    nc = ctx.nc

    x4_in = nc.dram_tensor("x4_in", [D, TLOC], F32, kind="ExternalInput").ap()
    q_in = nc.dram_tensor("q_in", [D, TLOC], BF16, kind="ExternalInput").ap()
    k_in = nc.dram_tensor("k_in", [KVD, SEQ], BF16, kind="ExternalInput").ap()
    vt_in = nc.dram_tensor("vt_in", [SEQ, KVD], BF16, kind="ExternalInput").ap()
    kposr_in = nc.dram_tensor("kposr_in", [1, TLOC], F32, kind="ExternalInput").ap()
    ktminr_in = nc.dram_tensor("ktminr_in", [1, 32], F32, kind="ExternalInput").ap()
    kqr_in = nc.dram_tensor("kqr_in", [1, 4], F32, kind="ExternalInput").ap()
    woT = nc.dram_tensor("woT", [D, D], F32, kind="ExternalInput").ap()
    go_r = nc.dram_tensor("go_r", [1, D], F32, kind="ExternalInput").ap()
    wguT = nc.dram_tensor("wguT", [D, 2 * INTER], F32, kind="ExternalInput").ap()
    wdT = nc.dram_tensor("wdT", [INTER, D], F32, kind="ExternalInput").ap()
    pos_in = nc.dram_tensor("pos", [1, TLOC], F32, kind="ExternalInput").ap()
    ident_in = nc.dram_tensor("ident_in", [128, 128], F32, kind="ExternalInput").ap()

    x6_out = nc.dram_tensor("x6_out", [D, TLOC], F32, kind="ExternalOutput").ap()

    x5_d = nc.dram_tensor("x5_d", [D, TLOC], F32, kind="Internal")

    sb, ps = ctx.sb, ctx.ps
    ones_bf = sb("ones_bf", [128, 1], BF16)
    ones_row = sb("ones_row", [1, 128])
    ident = sb("ident", [128, 128])
    pos_row = sb("pos_row", [1, TLOC])
    eps1 = sb("eps1", [1, 1])
    epsr = sb("epsr", [1, 1])
    row_mu = sb("row_mu", [1, TLOC])
    row_e2 = sb("row_e2", [1, TLOC])
    row_r = sb("row_r", [1, TLOC])
    mu_b = sb("mu_b", [128, TLOC])
    r_b = sb("r_b", [128, TLOC])
    scr = sb("scr", [128, TLOC])
    scr2 = sb("scr2", [128, TLOC])
    rrow = sb("rrow", [1, 128])
    sc1 = sb("sc1", [1, 1])
    m8a = sb("m8a", [128, 8])
    s8a = sb("s8a", [128, 8])
    mq4 = sb("mq4", [128, 4])
    sq4 = sb("sq4", [128, 4])
    mcol = sb("mcol", [128, 1])
    nmcol = sb("nmcol", [128, 1])
    f8 = sb("f8", [128, 8])
    f_row = sb("f_row", [1, TLOC])
    fb1 = sb("fb1", [128, 512])
    c1c = sb("c1c", [128, 1])
    phat1 = sb("phat1", [128, 512], BF16)
    qpos_c = sb("qpos_c", [128, 8])
    killr = sb("killr", [1, 4])
    kill_b = sb("kill_b", [128, 4])
    ktmin = sb("ktmin", [1, 32])
    indr = sb("indr", [1, 32])
    bias32 = sb("bias32", [128, 32])
    qmax1 = sb("qmax1", [1, 1])
    qcmax = sb("qcmax", [1, 1])

    def guard(name, shape, dt=F32):
        return nc.sbuf_tensor(name, shape, dt)

    pS1 = ps("pS1", [128, TLOC])
    pS2 = ps("pS2", [128, TLOC])
    pO1 = ps("pO1", [128, 512])
    pO2 = ps("pO2", [128, 512])
    pX = ps("pX", [128, 512])
    pY = ps("pY", [128, 512])

    w = dict(pC=pS1, pD=pS2, ones_bf=ones_bf, ones_row=ones_row,
             row_mu=row_mu, row_e2=row_e2, row_r=row_r, mu_b=mu_b, r_b=r_b,
             scr=scr, eps1=eps1)

    with nc.Block() as blk:
        ctx.blk = blk

        @blk.sync
        def _(sy):
            ctx.dma("sl", sy.dma_start(out=ident[:], in_=ident_in))
            ctx.dma("sl", sy.dma_start(out=pos_row[:], in_=pos_in))

        @blk.vector
        def _(v):
            ctx.inc("cv", v.memset(ones_bf[:], 1.0))
            ctx.chain(v, "cv", v.memset(ones_row[:], 1.0))
            ctx.chain(v, "cv", v.memset(eps1[:, :], LN_EPS))
            ctx.chain(v, "cv", v.memset(epsr[:, :], RMS_EPS))

        es_ob = ExitStack()
        obuf = es_ob.enter_context(nc.sbuf_tensor("obuf", [128, NCH * TLOC], BF16, side="right"))
        es_at = ExitStack()
        q_sb = es_at.enter_context(guard("q_sb", [128, NCH * TLOC], BF16))
        kall = es_at.enter_context(guard("kall", [128, 4 * SEQ], BF16))
        vall = es_at.enter_context(guard("vall", [128, 4 * SEQ], BF16))
        kpr = es_at.enter_context(guard("kpr", [1, TLOC]))
        kb0 = es_at.enter_context(guard("kb0", [128, TLOC]))
        qb_all = es_at.enter_context(guard("qb_all", [128, TLOC]))
        mk01 = es_at.enter_context(guard("mk01", [128, 4 * 512], BF16))
        exp_scr = scr

        @blk.sync
        def _(sy):
            ctx.wait(sy, "cv")
            ctx.dma("sl", sy.dma_start(out=q_sb.ap().rearrange("p (c t) -> p c t", t=TLOC),
                                       in_=q_in.rearrange("(c p) t -> p c t", p=128)))
            ctx.dma("sl", sy.dma_start(out=kall.ap().rearrange("p (c t) -> p c t", t=SEQ),
                                       in_=k_in.rearrange("(c p) t -> p c t", p=128)))
            ctx.dma("sl", sy.dma_start(
                out=vall.ap().rearrange("p (g kt d) -> p g kt d", g=4, d=128),
                in_=vt_in.rearrange("(kt p) (g d) -> p g kt d", p=128, d=128)))
            ctx.dma("sl", sy.dma_start(out=kpr[:], in_=kposr_in))
            ctx.dma("sl", sy.dma_start(out=ktmin[:], in_=ktminr_in))
            ctx.dma("sl", sy.dma_start(out=killr[:], in_=kqr_in))

        @blk.tensor
        def _(t):
            ctx.wait(t, "sl")
            for nn in range(2):
                ctx.inc("ct", t.matmul(pS1[:, nn * 512:(nn + 1) * 512], ones_row[0:1, :],
                                       kpr[0:1, nn * 512:(nn + 1) * 512], start=True, stop=True))
                ctx.inc("ct", t.matmul(pS2[:, nn * 512:(nn + 1) * 512], ones_row[0:1, :],
                                       pos_row[0:1, nn * 512:(nn + 1) * 512], start=True, stop=True))

        @blk.scalar
        def _(s):
            ctx.wait(s, "ct")
            ctx.chain(s, "cs", s.copy(kb0[:, :], pS1[:, 0:TLOC]))
            ctx.chain(s, "cs", s.copy(qb_all[:, :], pS2[:, 0:TLOC]))

        for tau in range(8):
            @blk.tensor
            def _(t, tau=tau):
                ctx.wait(t, "cs", "cv")
                ctx.inc("ct", t.matmul(pX[:, 0:1], pos_row[0:1, tau * 128:(tau + 1) * 128],
                                       ones_row[0:1, 0:1], start=True, stop=True))

            @blk.scalar
            def _(s, tau=tau):
                ctx.wait(s, "ct")
                ctx.chain(s, "cs", s.copy(qpos_c[:, tau:tau + 1], pX[:, 0:1]))

        @blk.vector
        def _(v):
            ctx.wait(v, "cs")
            ctx.wait(v, "sl")
            ctx.chain(v, "cv", v.tensor_copy(out=qmax1[:, :], in_=pos_row[0:1, TLOC - 1:TLOC]))
            ctx.chain(v, "cv", v.tensor_scalar(out=killr[:, :], in0=killr[:, :],
                                               scalar1=qmax1[0:1, 0:1], scalar2=NEG,
                                               op0=ALU.is_gt, op1=ALU.mult))

        @blk.tensor
        def _(t):
            ctx.wait(t, "cv")
            ctx.inc("ct", t.matmul(pX[:, 0:4], ones_row[0:1, :], killr[0:1, 0:4],
                                   start=True, stop=True))

        @blk.scalar
        def _(s):
            ctx.wait(s, "ct")
            ctx.chain(s, "cs", s.copy(kill_b[:, :], pX[:, 0:4]))

        es_m1 = ExitStack()
        mks = es_m1.enter_context(guard("mks", [128, 8 * TLOC], BF16))
        for tau in range(8):
            @blk.vector
            def _(v, tau=tau):
                ctx.wait(v, "cs", "ct")
                ctx.chain(v, "cv", v.tensor_scalar(
                    out=mks[:, tau * TLOC:(tau + 1) * TLOC], in0=kb0[:, :],
                    scalar1=qpos_c[:, tau:tau + 1], scalar2=NEG,
                    op0=ALU.is_gt, op1=ALU.mult))

        for h in range(16):
            g_ = h // 4
            kvb = g_ * SEQ

            for tau in range(8):
                for qq in range(4):
                    @blk.tensor
                    def _(t, h=h, tau=tau, qq=qq, kvb=kvb):
                        ctx.wait(t, "sl", "cs", "cv")
                        for kc in range(2):
                            ctx.inc("ct", t.matmul(
                                pS1[:, kc * 512:(kc + 1) * 512],
                                q_sb[:, h * TLOC + tau * 128: h * TLOC + (tau + 1) * 128],
                                kall[:, kvb + qq * TLOC + kc * 512: kvb + qq * TLOC + (kc + 1) * 512],
                                start=True, stop=True))

                    if qq == 0:
                        @blk.vector
                        def _(v, tau=tau):
                            ctx.wait(v, "ct")
                            ctx.chain(v, "cv", v.tensor_add(pS1[:, 0:TLOC], pS1[:, 0:TLOC],
                                                            mks[:, tau * TLOC:(tau + 1) * TLOC]))

                    @blk.vector
                    def _(v, qq=qq):
                        ctx.wait(v, "ct")
                        ctx.chain(v, "cv", v.reduce_max(mq4[:, qq:qq + 1], pS1[:, 0:TLOC],
                                                        axis=AX.X))
                        ctx.chain(v, "cv", v.scalar_tensor_tensor(
                            out=nmcol[:, :], in0=mq4[:, qq:qq + 1], scalar=-0.125,
                            in1=kill_b[:, qq:qq + 1], op0=ALU.mult, op1=ALU.add)
                            if False else
                            v.tensor_scalar(out=nmcol[:, :], in0=mq4[:, qq:qq + 1],
                                            scalar1=-0.08838834764831845, scalar2=None, op0=ALU.mult))
                        ctx.chain(v, "cv", v.tensor_add(nmcol[:, :], nmcol[:, :],
                                                        kill_b[:, qq:qq + 1]))

                    @blk.scalar
                    def _(s, qq=qq):
                        ctx.wait(s, "cv")
                        ctx.chain(s, "cs", s.activation(out=exp_scr[:, :], in_=pS1[:, 0:TLOC],
                                                        func=AF.Exp, scale=0.08838834764831845,
                                                        bias=nmcol[:, 0:1],
                                                        accum_out=sq4[:, qq:qq + 1]))

                @blk.vector
                def _(v, tau=tau):
                    ctx.wait(v, "cs")
                    ctx.chain(v, "cv", v.tensor_scalar(out=mq4[:, :], in0=mq4[:, :],
                                                       scalar1=0.08838834764831845, scalar2=None, op0=ALU.mult))
                    ctx.chain(v, "cv", v.tensor_add(mq4[:, :], mq4[:, :], kill_b[:, :]))
                    ctx.chain(v, "cv", v.reduce_max(mcol[:, :], mq4[:, :], axis=AX.X))
                    ctx.chain(v, "cv", v.tensor_copy(out=m8a[:, tau:tau + 1], in_=mcol[:, :]))
                    ctx.chain(v, "cv", v.tensor_scalar(out=mq4[:, :], in0=mq4[:, :],
                                                       scalar1=mcol[:, 0:1], scalar2=None,
                                                       op0=ALU.subtract))

                @blk.scalar
                def _(s):
                    ctx.wait(s, "cv")
                    ctx.chain(s, "cs", s.activation(out=mq4[:, :], in_=mq4[:, :], func=AF.Exp))

                @blk.vector
                def _(v, tau=tau):
                    ctx.wait(v, "cs")
                    ctx.chain(v, "cv", v.tensor_mul(mq4[:, :], mq4[:, :], sq4[:, :]))
                    ctx.chain(v, "cv", v.reduce_sum(s8a[:, tau:tau + 1], mq4[:, :], axis=AX.X))

            @blk.vector
            def _(v):
                ctx.wait(v, "cs", "ct")
                ctx.chain(v, "cv", v.reduce_max(mcol[:, :], m8a[:, :], axis=AX.X))

            @blk.tensor
            def _(t):
                ctx.wait(t, "cv")
                ctx.inc("ct", t.matmul(pX[0:1, 0:128], mcol[:, 0:1], ident[:],
                                       start=True, stop=True))

            @blk.scalar
            def _(s):
                ctx.wait(s, "ct")
                ctx.chain(s, "cs", s.copy(rrow[:, :], pX[0:1, 0:128]))

            @blk.vector
            def _(v):
                ctx.wait(v, "cs")
                ctx.chain(v, "cv", v.reduce_max(sc1[:, :], rrow[:, :], axis=AX.X))

            @blk.tensor
            def _(t):
                ctx.wait(t, "cv")
                ctx.inc("ct", t.matmul(pX[:, 0:1], ones_row[0:1, :], sc1[0:1, 0:1],
                                       start=True, stop=True))

            @blk.scalar
            def _(s):
                ctx.wait(s, "ct")
                ctx.chain(s, "cs", s.copy(c1c[:, :], pX[:, 0:1]))
                ctx.chain(s, "cs", s.activation(out=f8[:, :], in_=s8a[:, :], func=AF.Ln))

            @blk.vector
            def _(v):
                ctx.wait(v, "cs")
                ctx.chain(v, "cv", v.tensor_add(f8[:, :], f8[:, :], m8a[:, :]))

            @blk.scalar
            def _(s):
                ctx.wait(s, "cv")
                ctx.chain(s, "cs", s.activation(out=f8[:, :], in_=f8[:, :], func=AF.Exp,
                                                scale=-1.0, bias=c1c[:, 0:1]))

            for tau in range(8):
                @blk.tensor
                def _(t, tau=tau):
                    ctx.wait(t, "cs")
                    ctx.inc("ct", t.matmul(pX[0:1, 0:128], f8[:, tau:tau + 1], ident[:],
                                           start=True, stop=True))

                @blk.scalar
                def _(s, tau=tau):
                    ctx.wait(s, "ct")
                    ctx.chain(s, "cs", s.copy(f_row[0:1, tau * 128:(tau + 1) * 128],
                                              pX[0:1, 0:128]))

            for qc in range(2):
                @blk.vector
                def _(v, qc=qc):
                    ctx.wait(v, "cs", "ct")
                    ctx.chain(v, "cv", v.tensor_copy(out=qcmax[:, :],
                                                     in_=pos_row[0:1, qc * 512 + 511:qc * 512 + 512]))
                    ctx.chain(v, "cv", v.tensor_scalar(out=indr[:, :], in0=ktmin[:, :],
                                                       scalar1=qcmax[0:1, 0:1], scalar2=NEG,
                                                       op0=ALU.is_gt, op1=ALU.mult))
                    ctx.chain(v, "cv", v.tensor_scalar(out=indr[:, :], in0=indr[:, :],
                                                       scalar1=sc1[0:1, 0:1], scalar2=None,
                                                       op0=ALU.subtract))

                @blk.tensor
                def _(t, qc=qc):
                    ctx.wait(t, "cv")
                    ctx.inc("ct", t.matmul(pX[:, 0:32], ones_row[0:1, :], indr[0:1, :],
                                           start=True, stop=True))
                    ctx.inc("ct", t.matmul(pY[:, 0:512], ones_row[0:1, :],
                                           f_row[0:1, qc * 512:(qc + 1) * 512],
                                           start=True, stop=True))

                @blk.scalar
                def _(s):
                    ctx.wait(s, "ct")
                    ctx.chain(s, "cs", s.copy(bias32[:, :], pX[:, 0:32]))
                    ctx.chain(s, "cs", s.copy(fb1[:, :], pY[:, 0:512]))

                for mi in range(4):
                    kt = 4 * qc + mi

                    @blk.tensor
                    def _(t, kt=kt):
                        ctx.wait(t, "cv", "cs", "cg")
                        ctx.inc("ct", t.matmul(pX[:, 0:1], kpr[0:1, kt * 128:(kt + 1) * 128],
                                               ones_row[0:1, 0:1], start=True, stop=True))

                    @blk.scalar
                    def _(s):
                        ctx.wait(s, "ct")
                        ctx.chain(s, "cs", s.copy(mcol[:, :], pX[:, 0:1]))

                    @blk.vector
                    def _(v, qc=qc, mi=mi):
                        ctx.wait(v, "cs")
                        ctx.chain(v, "cv", v.tensor_scalar(
                            out=mk01[:, mi * 512:(mi + 1) * 512],
                            in0=qb_all[:, qc * 512:(qc + 1) * 512],
                            scalar1=mcol[:, 0:1], scalar2=None, op0=ALU.is_ge))

                for kt in range(32):
                    @blk.tensor
                    def _(t, h=h, qc=qc, kt=kt, kvb=kvb):
                        ctx.wait(t, "cs", "cv", "cg")
                        ctx.inc("ct", t.matmul(
                            pS1[:, 0:512],
                            kall[:, kvb + kt * 128: kvb + (kt + 1) * 128],
                            q_sb[:, h * TLOC + qc * 512: h * TLOC + (qc + 1) * 512],
                            start=True, stop=True))

                    @blk.scalar
                    def _(s, kt=kt):
                        ctx.wait(s, "ct")
                        ctx.chain(s, "cs", s.activation(out=phat1[:, :], in_=pS1[:, 0:512],
                                                        func=AF.Exp, scale=0.08838834764831845,
                                                        bias=bias32[:, kt:kt + 1]))

                    if kt < 8 and kt // 4 == qc:
                        mi = kt % 4

                        @blk.gpsimd
                        def _(g, mi=mi):
                            ctx.wait(g, "cs")
                            ctx.inc("cg", g.tensor_mul(phat1[:, :], phat1[:, :],
                                                       mk01[:, mi * 512:(mi + 1) * 512]))

                    @blk.tensor
                    def _(t, kt=kt, kvb=kvb):
                        ctx.wait(t, "cs", "cg")
                        ctx.inc("ct", t.matmul(pO1[:, 0:512],
                                               vall[:, kvb + kt * 128: kvb + (kt + 1) * 128],
                                               phat1[:, :], start=(kt == 0), stop=(kt == 31)))

                @blk.vector
                def _(v, h=h, qc=qc):
                    ctx.wait(v, "ct")
                    ctx.chain(v, "cv", v.tensor_mul(
                        obuf[:, h * TLOC + qc * 512: h * TLOC + (qc + 1) * 512],
                        pO1[:, 0:512], fb1[:, :]))

        es_m1.close()
        es_at.close()

        es_x5 = ExitStack()
        x5_bf = es_x5.enter_context(guard("x5_bf", [128, NCH * TLOC], BF16))
        es_qd = ExitStack()
        gb_o = es_qd.enter_context(guard("gb_o", [128, 2048]))
        gcol_o = es_qd.enter_context(guard("gcol_o", [128, NCH]))
        xq = es_qd.enter_context(guard("xq", [128, NCH * 512], BF16))
        arow = es_qd.enter_context(guard("arow", [1, TLOC]))
        ainvrow = es_qd.enter_context(guard("ainvrow", [1, TLOC]))
        sc_b = es_qd.enter_context(guard("sc_b", [128, 512]))
        a_b = es_qd.enter_context(guard("a_b", [128, 512]))
        rstd_b = es_qd.enter_context(guard("rstd_b", [128, TLOC]))
        ag1 = es_qd.enter_context(guard("ag1", [128, 8]))
        ws1 = es_qd.enter_context(guard("ws1", [1, 8]))
        wsc = es_qd.enter_context(guard("wsc", [128, 1]))
        qsq = es_qd.enter_context(guard("qsq", [128, TLOC], BF16))
        xr_tm = es_qd.enter_context(guard("xr_tm", [128, 2048]))
        tern = es_qd.enter_context(guard("tern", [128, NCH * 128], BF16))
        tpos = es_qd.enter_context(guard("tpos", [128, NCH * 128], BF16))
        wcol = es_qd.enter_context(guard("wcol", [128, NCH * 128], BF16))
        z_v = es_qd.enter_context(guard("z_v", [128, 512]))
        u_v = es_qd.enter_context(guard("u_v", [128, 512]))
        b_v = es_qd.enter_context(guard("b_v", [128, 512]))
        z_g = es_qd.enter_context(guard("z_g", [128, 512]))
        u_g = es_qd.enter_context(guard("u_g", [128, 512]))
        b_g = es_qd.enter_context(guard("b_g", [128, 512]))

        @blk.gpsimd
        def _(g):
            ctx.wait(g, "cv", "cs", "ct")
            ctx.dma("gl", g.dma_start(out=row_mu.ap()[0:1, 0:TLOC], in_=go_r[0:1, 0:TLOC]))
            ctx.dma("gl", g.dma_start(out=row_e2.ap()[0:1, 0:TLOC], in_=go_r[0:1, TLOC:D]))

        @blk.tensor
        def _(t):
            ctx.wait(t, "gl")
            for q4 in range(2):
                ctx.inc("ct", t.matmul(pS1[:, q4 * 512:(q4 + 1) * 512], ones_row[0:1, :],
                                       row_mu[0:1, q4 * 512:(q4 + 1) * 512], start=True, stop=True))
                ctx.inc("ct", t.matmul(pS2[:, q4 * 512:(q4 + 1) * 512], ones_row[0:1, :],
                                       row_e2[0:1, q4 * 512:(q4 + 1) * 512], start=True, stop=True))

        @blk.scalar
        def _(s):
            ctx.wait(s, "ct")
            ctx.chain(s, "cs", s.copy(gb_o[:, 0:TLOC], pS1[:, 0:TLOC]))
            ctx.chain(s, "cs", s.copy(gb_o[:, TLOC:2048], pS2[:, 0:TLOC]))

        for ci in range(NCH):
            @blk.tensor
            def _(t, ci=ci):
                ctx.wait(t, "cs")
                ctx.inc("ct", t.transpose(pX[:, 0:128], gb_o[:, ci * 128:(ci + 1) * 128],
                                          ident[:]))

            @blk.scalar
            def _(s, ci=ci):
                ctx.wait(s, "ct")
                ctx.chain(s, "cs", s.copy(gcol_o[:, ci:ci + 1], pX[:, 0:1]))

        for ci in range(NCH):
            @blk.scalar
            def _(s, ci=ci):
                ctx.wait(s, "cv")
                if ci > 0:
                    ctx.wait(s, "ct")
                ctx.chain(s, "cs", s.activation(out=qsq[:, :],
                                                in_=obuf[:, ci * TLOC:(ci + 1) * TLOC],
                                                func=AF.Square))

            @blk.tensor
            def _(t, ci=ci):
                ctx.wait(t, "cs")
                for nn in range(2):
                    ctx.inc("ct", t.matmul(pS2[0:1, nn * 512:(nn + 1) * 512], ones_bf[:, 0:1],
                                           qsq[:, nn * 512:(nn + 1) * 512],
                                           start=(ci == 0), stop=(ci == NCH - 1),
                                           skip_group_check=True))

        @blk.scalar
        def _(s):
            ctx.wait(s, "ct")
            ctx.chain(s, "cs", s.activation(out=row_r[:, :], in_=pS2[0:1, 0:TLOC], func=AF.Sqrt,
                                            scale=1.0 / D, bias=epsr[0:1, 0:1]))

        @blk.vector
        def _(v):
            ctx.wait(v, "cs")
            ctx.chain(v, "cv", v.reciprocal(row_r[:, :], row_r[:, :]))

        @blk.tensor
        def _(t):
            ctx.wait(t, "cv")
            for nn in range(2):
                ctx.inc("ct", t.matmul(pS1[:, nn * 512:(nn + 1) * 512], ones_row[0:1, :],
                                       row_r[0:1, nn * 512:(nn + 1) * 512], start=True, stop=True))

        @blk.scalar
        def _(s):
            ctx.wait(s, "ct")
            ctx.chain(s, "cs", s.copy(rstd_b[:, :], pS1[:, 0:TLOC]))

        for tau in range(8):
            for ci in range(NCH):
                @blk.vector
                def _(v, tau=tau, ci=ci):
                    ctx.wait(v, "cs", "ct")
                    ctx.chain(v, "cv", v.tensor_mul(
                        scr[:, 0:128],
                        obuf[:, ci * TLOC + tau * 128: ci * TLOC + (tau + 1) * 128],
                        rstd_b[:, tau * 128:(tau + 1) * 128]))

                @blk.tensor
                def _(t, ci=ci):
                    ctx.wait(t, "cv")
                    ctx.inc("ct", t.transpose(pX[:, 0:128], scr[:, 0:128], ident[:]))

                @blk.scalar
                def _(s, ci=ci):
                    ctx.wait(s, "ct")
                    ctx.chain(s, "cs", s.copy(xr_tm[:, ci * 128:(ci + 1) * 128], pX[:, 0:128]))

            @blk.vector
            def _(v, tau=tau):
                ctx.wait(v, "cs")
                ctx.chain(v, "cv", v.tensor_mul(scr2[:, 0:TLOC], xr_tm[:, 0:TLOC],
                                                gb_o[:, 0:TLOC]))
                ctx.chain(v, "cv", v.tensor_mul(scr[:, 0:TLOC], xr_tm[:, TLOC:2048],
                                                gb_o[:, TLOC:2048]))
                ctx.chain(v, "cv", v.reduce_max(mcol[:, :], scr2[:, 0:TLOC], axis=AX.X,
                                                apply_absolute_value=True))
                ctx.chain(v, "cv", v.reduce_max(nmcol[:, :], scr[:, 0:TLOC], axis=AX.X,
                                                apply_absolute_value=True))
                ctx.chain(v, "cv", v.tensor_max(ag1[:, tau:tau + 1], mcol[:, :], nmcol[:, :]))

        @blk.vector
        def _(v):
            ctx.wait(v, "cs", "ct")
            ctx.chain(v, "cv", v.tensor_scalar(out=ag1[:, :], in0=ag1[:, :],
                                               scalar1=1e-5, scalar2=None, op0=ALU.max))

        @blk.vector
        def _(v):
            ctx.chain(v, "cv", v.memset(wsc[:, :], 0.0))

        for pp in range(16):
            @blk.gpsimd
            def _(g, pp=pp):
                ctx.wait(g, "cv", "ct", "cs")
                ctx.dma("gl", g.dma_start(
                    out=wcol.ap().rearrange("p (c m) -> p c m", m=128),
                    in_=woT.rearrange("(c p) m -> p c m", p=128)[:, :, pp * 128:(pp + 1) * 128]))

            @blk.vector
            def _(v):
                ctx.wait(v, "gl")
                ctx.chain(v, "cv", v.tensor_reduce(mcol[:, 0:1], wcol[:, :], axis=AX.X,
                                                   op=ALU.add, apply_absolute_value=True))
                ctx.chain(v, "cv", v.tensor_add(wsc[:, 0:1], wsc[:, 0:1], mcol[:, 0:1]))

        @blk.tensor
        def _(t):
            ctx.wait(t, "cv")
            ctx.inc("ct", t.matmul(pX[0:1, 0:128], wsc[:, 0:1], ident[:], start=True, stop=True))

        @blk.scalar
        def _(s):
            ctx.wait(s, "ct")
            ctx.chain(s, "cs", s.copy(rrow[:, :], pX[0:1, 0:128]))

        @blk.vector
        def _(v):
            ctx.wait(v, "cs")
            ctx.chain(v, "cv", v.reduce_sum(ws1[0:1, 0:1], rrow[:, :], axis=AX.X))
            ctx.chain(v, "cv", v.tensor_scalar(out=ws1[0:1, 0:1], in0=ws1[0:1, 0:1],
                                               scalar1=1.0 / (D * D), scalar2=1e-5,
                                               op0=ALU.mult, op1=ALU.max))
            ctx.chain(v, "cv", v.reciprocal(ws1[0:1, 1:2], ws1[0:1, 0:1]))

        @blk.tensor
        def _(t):
            ctx.wait(t, "cv")
            ctx.inc("ct", t.matmul(pX[:, 0:1], ones_row[0:1, :], ws1[0:1, 1:2],
                                   start=True, stop=True))

        @blk.scalar
        def _(s):
            ctx.wait(s, "ct")
            ctx.chain(s, "cs", s.copy(wsc[:, 0:1], pX[:, 0:1]))

        STAIR = [128.0, 64.0, 32.0, 16.0, 8.0, 4.0, 2.0, 1.0]
        for nn in range(2):
            hb = nn * 512
            for tau in range(4):
                tg = nn * 4 + tau

                @blk.tensor
                def _(t, tg=tg):
                    ctx.wait(t, "cv", "cs")
                    ctx.inc("ct", t.matmul(pX[0:1, 0:128], ag1[:, tg:tg + 1], ident[:],
                                           start=True, stop=True))

                @blk.scalar
                def _(s, tau=tau, hb=hb):
                    ctx.wait(s, "ct")
                    ctx.chain(s, "cs", s.copy(arow[0:1, hb + tau * 128: hb + (tau + 1) * 128],
                                              pX[0:1, 0:128]))

            @blk.vector
            def _(v, hb=hb):
                ctx.wait(v, "cs")
                ctx.chain(v, "cv", v.tensor_scalar(out=ainvrow[0:1, hb:hb + 512],
                                                   in0=arow[0:1, hb:hb + 512],
                                                   scalar1=1.0 / 127.0, scalar2=None, op0=ALU.mult))
                ctx.chain(v, "cv", v.reciprocal(arow[0:1, hb:hb + 512], arow[0:1, hb:hb + 512]))
                ctx.chain(v, "cv", v.tensor_scalar(out=arow[0:1, hb:hb + 512],
                                                   in0=arow[0:1, hb:hb + 512],
                                                   scalar1=127.0, scalar2=None, op0=ALU.mult))
                ctx.chain(v, "cv", v.tensor_scalar(out=ainvrow[0:1, hb:hb + 512],
                                                   in0=ainvrow[0:1, hb:hb + 512],
                                                   scalar1=ws1[0:1, 0:1], scalar2=None,
                                                   op0=ALU.mult))

            @blk.tensor
            def _(t, hb=hb):
                ctx.wait(t, "cv")
                ctx.inc("ct", t.matmul(pX[:, 0:512], ones_row[0:1, :], arow[0:1, hb:hb + 512],
                                       start=True, stop=True))
                ctx.inc("ct", t.matmul(pY[:, 0:512], ones_row[0:1, :], ainvrow[0:1, hb:hb + 512],
                                       start=True, stop=True))

            @blk.scalar
            def _(s):
                ctx.wait(s, "ct")
                ctx.chain(s, "cs", s.copy(a_b[:, 0:512], pX[:, 0:512]))
                ctx.chain(s, "cs", s.copy(sc_b[:, 0:512], pY[:, 0:512]))

            for ci in range(NCH):
                if ci % 2 == 0:
                    sem, zT, uT, bT = "cv", z_v, u_v, b_v
                else:
                    sem, zT, uT, bT = "cg", z_g, u_g, b_g

                def quant_body(E, ci=ci, hb=hb, sem=sem, zT=zT, uT=uT, bT=bT):
                    ctx.wait(E, "cs", "ct")
                    if sem == "cv":
                        ctx.wait(E, "cg")
                    else:
                        ctx.wait(E, "cv", "gl")
                    ctx.chain(E, sem, E.tensor_mul(zT[:, :],
                                                   obuf[:, ci * TLOC + hb: ci * TLOC + hb + 512],
                                                   rstd_b[:, hb:hb + 512]))
                    ctx.chain(E, sem, E.tensor_scalar(out=zT[:, :], in0=zT[:, :],
                                                      scalar1=gcol_o[:, ci:ci + 1],
                                                      scalar2=None, op0=ALU.mult))
                    ctx.chain(E, sem, E.tensor_mul(zT[:, :], zT[:, :], a_b[:, 0:512]))
                    ctx.chain(E, sem, E.tensor_scalar(out=uT[:, :], in0=zT[:, :],
                                                      scalar1=128.5, scalar2=None, op0=ALU.add))
                    for wv_ in STAIR:
                        ctx.chain(E, sem, E.tensor_scalar(out=bT[:, :], in0=uT[:, :],
                                                          scalar1=wv_, scalar2=wv_,
                                                          op0=ALU.is_ge, op1=ALU.mult))
                        ctx.chain(E, sem, E.tensor_sub(uT[:, :], uT[:, :], bT[:, :]))
                    ctx.chain(E, sem, E.tensor_sub(zT[:, :], zT[:, :], uT[:, :]))
                    ctx.chain(E, sem, E.tensor_scalar(out=xq[:, ci * 512:(ci + 1) * 512],
                                                      in0=zT[:, :], scalar1=0.5, scalar2=None,
                                                      op0=ALU.add))

                if sem == "cv":
                    @blk.vector
                    def _(v, qb=quant_body):
                        qb(v)
                else:
                    @blk.gpsimd
                    def _(g, qb=quant_body):
                        qb(g)

            for mo in range(NCH):
                @blk.gpsimd
                def _(g, mo=mo):
                    ctx.wait(g, "ct", "cv", "cs")
                    ctx.dma("gl", g.dma_start(
                        out=wcol.ap().rearrange("p (c m) -> p c m", m=128),
                        in_=woT.rearrange("(c p) m -> p c m", p=128)[:, :, mo * 128:(mo + 1) * 128]))

                @blk.vector
                def _(v):
                    ctx.wait(v, "gl", "ct")
                    ctx.chain(v, "cv", v.tensor_scalar(out=tpos[:, :], in0=wcol[:, :],
                                                       scalar1=wsc[:, 0:1], scalar2=0.5,
                                                       op0=ALU.mult, op1=ALU.is_ge))
                    ctx.chain(v, "cv", v.tensor_scalar(out=tern[:, :], in0=wcol[:, :],
                                                       scalar1=wsc[:, 0:1], scalar2=-0.5,
                                                       op0=ALU.mult, op1=ALU.is_le))
                    ctx.chain(v, "cv", v.tensor_sub(tern[:, :], tpos[:, :], tern[:, :]))

                @blk.tensor
                def _(t, mo=mo):
                    ctx.wait(t, "cv", "cg", "cs")
                    for ci in range(NCH):
                        ctx.inc("ct", t.matmul(
                            pO1[:, 0:512], tern[:, ci * 128:(ci + 1) * 128],
                            xq[:, ci * 512:(ci + 1) * 512],
                            start=(ci == 0), stop=(ci == NCH - 1)))

                @blk.sync
                def _(sy, mo=mo, hb=hb):
                    ctx.wait(sy, "cv")
                    ctx.dma("sl", sy.dma_start(
                        out=scr2[:, 0:512],
                        in_=x4_in.rearrange("(c p) t -> c p t", p=128)[mo][:, hb:hb + 512]))

                @blk.vector
                def _(v, mo=mo, hb=hb):
                    ctx.wait(v, "ct", "sl", "so")
                    ctx.chain(v, "cv", v.tensor_mul(scr[:, 0:512], pO1[:, 0:512], sc_b[:, 0:512]))
                    ctx.chain(v, "cv", v.tensor_add(scr[:, 0:512], scr[:, 0:512], scr2[:, 0:512]))
                    ctx.chain(v, "cv", v.tensor_copy(out=x5_bf[:, mo * TLOC + hb: mo * TLOC + hb + 512],
                                                     in_=scr[:, 0:512]))

                @blk.sync
                def _(sy, mo=mo, hb=hb):
                    ctx.wait(sy, "cv")
                    ctx.dma("so", sy.dma_start(
                        out=x5_d.ap().rearrange("(c p) t -> c p t", p=128)[mo][:, hb:hb + 512],
                        in_=scr[:, 0:512]))

        es_qd.close()
        es_ob.close()
        es_f1 = ExitStack()
        xln = es_f1.enter_context(guard("xln", [128, NCH * TLOC], BF16))
        es_f2 = ExitStack()
        sqb = es_f2.enter_context(guard("sqb", [128, NCH * TLOC], BF16))
        w["sq"] = sqb
        emit_ln(ctx, lambda ci: x5_bf[:, ci * TLOC:(ci + 1) * TLOC],
                lambda ci: xln[:, ci * TLOC:(ci + 1) * TLOC], w)
        es_f2.close()

        x6a_d = nc.dram_tensor("x6a_d", [D, TLOC], F32, kind="Internal")
        es_f3 = ExitStack()
        h_i = es_f3.enter_context(guard("h_i", [128, NCH * TLOC], BF16))
        fcol = es_f3.enter_context(guard("fcol", [128, 2 * NCH * 128], BF16))

        for ic in range(4):
            for mo2 in range(NCH):
                slot = mo2 % 2
                cb = slot * NCH * 128

                @blk.gpsimd
                def _(g, ic=ic, mo2=mo2, cb=cb):
                    ctx.wait(g, "ct", "cv", "cs")
                    ctx.dma("gl", g.dma_start(
                        out=fcol.ap()[:, cb:cb + NCH * 128].rearrange("p (c m) -> p c m", m=128),
                        in_=wguT.rearrange("(c p) m -> p c m", p=128)
                            [:, :, ic * 2048 + mo2 * 128: ic * 2048 + (mo2 + 1) * 128]))

                @blk.tensor
                def _(t, cb=cb):
                    ctx.wait(t, "gl", "cs", "cv")
                    for nn in range(2):
                        pdst = pO1 if nn == 0 else pO2
                        for ci in range(NCH):
                            ctx.inc("ct", t.matmul(
                                pdst[:, 0:512], fcol[:, cb + ci * 128:cb + (ci + 1) * 128],
                                xln[:, ci * TLOC + nn * 512: ci * TLOC + (nn + 1) * 512],
                                start=(ci == 0), stop=(ci == NCH - 1)))

                @blk.gpsimd
                def _(g, ic=ic, mo2=mo2, cb=cb):
                    ctx.wait(g, "ct")
                    ctx.dma("gl", g.dma_start(
                        out=fcol.ap()[:, cb:cb + NCH * 128].rearrange("p (c m) -> p c m", m=128),
                        in_=wguT.rearrange("(c p) m -> p c m", p=128)
                            [:, :, INTER + ic * 2048 + mo2 * 128: INTER + ic * 2048 + (mo2 + 1) * 128]))

                @blk.scalar
                def _(s):
                    ctx.wait(s, "ct")
                    ctx.chain(s, "cs", s.activation(out=scr[:, 0:512], in_=pO1[:, 0:512], func=AF.Silu))
                    ctx.chain(s, "cs", s.activation(out=scr[:, 512:TLOC], in_=pO2[:, 0:512], func=AF.Silu))

                @blk.tensor
                def _(t, cb=cb):
                    ctx.wait(t, "gl", "cs")
                    for nn in range(2):
                        pdst = pO1 if nn == 0 else pO2
                        for ci in range(NCH):
                            ctx.inc("ct", t.matmul(
                                pdst[:, 0:512], fcol[:, cb + ci * 128:cb + (ci + 1) * 128],
                                xln[:, ci * TLOC + nn * 512: ci * TLOC + (nn + 1) * 512],
                                start=(ci == 0), stop=(ci == NCH - 1)))

                @blk.scalar
                def _(s):
                    ctx.wait(s, "ct")
                    ctx.chain(s, "cs", s.copy(scr2[:, 0:512], pO1[:, 0:512]))
                    ctx.chain(s, "cs", s.copy(scr2[:, 512:TLOC], pO2[:, 0:512]))

                @blk.vector
                def _(v, mo2=mo2):
                    ctx.wait(v, "cs")
                    ctx.chain(v, "cv", v.tensor_mul(h_i[:, mo2 * TLOC:(mo2 + 1) * TLOC],
                                                    scr[:, :], scr2[:, :]))

            for mo in range(NCH):
                slot = mo % 2
                cb = slot * NCH * 128

                @blk.gpsimd
                def _(g, ic=ic, mo=mo, cb=cb):
                    ctx.wait(g, "ct", "cv", "cs")
                    ctx.dma("gl", g.dma_start(
                        out=fcol.ap()[:, cb:cb + NCH * 128].rearrange("p (c m) -> p c m", m=128),
                        in_=wdT.rearrange("(c p) m -> p c m", p=128)
                            [:, ic * NCH:(ic + 1) * NCH, mo * 128:(mo + 1) * 128]))

                @blk.tensor
                def _(t, ic=ic, mo=mo, cb=cb):
                    ctx.wait(t, "gl", "cv", "cs")
                    for nn in range(2):
                        pdst = pO1 if nn == 0 else pO2
                        for ci in range(NCH):
                            ctx.inc("ct", t.matmul(
                                pdst[:, 0:512], fcol[:, cb + ci * 128:cb + (ci + 1) * 128],
                                h_i[:, ci * TLOC + nn * 512: ci * TLOC + (nn + 1) * 512],
                                start=(ci == 0), stop=(ci == NCH - 1)))

                @blk.sync
                def _(sy, ic=ic, mo=mo):
                    ctx.wait(sy, "cv")
                    src = x5_d if ic == 0 else x6a_d
                    ctx.dma("sl", sy.dma_start(
                        out=scr2[:, 0:TLOC],
                        in_=src.ap().rearrange("(c p) t -> c p t", p=128)[mo]))

                @blk.vector
                def _(v, mo=mo):
                    ctx.wait(v, "ct", "sl", "so")
                    for nn in range(2):
                        pdst = pO1 if nn == 0 else pO2
                        ctx.chain(v, "cv", v.scalar_tensor_tensor(
                            out=scr[:, nn * 512:(nn + 1) * 512],
                            in0=pdst[:, 0:512], scalar=0.5,
                            in1=scr2[:, nn * 512:(nn + 1) * 512],
                            op0=ALU.mult, op1=ALU.add))

                @blk.sync
                def _(sy, ic=ic, mo=mo):
                    ctx.wait(sy, "cv")
                    dst = x6a_d.ap() if ic < 3 else x6_out
                    ctx.dma("so", sy.dma_start(
                        out=dst.rearrange("(c p) t -> c p t", p=128)[mo], in_=scr[:, 0:TLOC]))

        @blk.sync
        def _(sy):
            ctx.wait(sy, "so")

        es_f3.close()
        es_f1.close()
        es_x5.close()

    ctx.es.close()
    return ctx



_CTX_CACHE = {}


def _get_ctx(name, builder):
    if name not in _CTX_CACHE:
        _CTX_CACHE[name] = builder()
    return _CTX_CACHE[name]


def _l0_perm():
    perm = np.zeros(2048, np.int64)
    for h in range(16):
        for v in range(2):
            for par in range(2):
                for i in range(32):
                    perm[128 * h + 64 * v + 32 * par + i] = 64 * (2 * h + v) + 2 * i + par
    return perm


def _l1_perm(nout):
    perm = np.zeros(nout, np.int64)
    for h in range(nout // 128):
        for half in range(2):
            for i in range(64):
                perm[128 * h + 64 * half + i] = 128 * h + 2 * i + half
    return perm


def kernel(**inputs):
    f32 = lambda k: np.asarray(inputs[k], np.float32)
    x = f32("L0_attn_x")
    ident = np.eye(128, dtype=np.float32)
    permL0 = _l0_perm()
    permq1 = _l1_perm(2048)
    permk1 = _l1_perm(512)

    slices = []
    for c in range(8):
        b, j = c // 4, c % 4
        slices.append(np.ascontiguousarray(x[b, j * 1024:(j + 1) * 1024, :].T))
    x_full = np.ascontiguousarray(np.stack(slices, 0))

    sharedA = dict(
        x_full=x_full,
        gate_wT=np.ascontiguousarray(f32("L0_ffn_gate_w").T),
        w1T=np.ascontiguousarray(np.transpose(f32("L0_ffn_w1"), (0, 2, 1))),
        w3T=np.ascontiguousarray(np.transpose(f32("L0_ffn_w3"), (0, 2, 1))),
        w2T=np.ascontiguousarray(np.transpose(f32("L0_ffn_w2"), (0, 2, 1))),
        wqT=np.ascontiguousarray(f32("L0_attn_Wq")[permL0].T),
        wkT=np.ascontiguousarray(f32("L0_attn_Wk")[permL0].T),
        wvT=np.ascontiguousarray(f32("L0_attn_Wv").T),
        ident_in=ident,
    )
    in_maps = []
    for c in range(8):
        m = dict(sharedA)
        m["x_own"] = slices[c]
        m["pos"] = np.arange((c % 4) * 1024, (c % 4) * 1024 + 1024, dtype=np.float32)[None, :]
        oh = np.zeros((1, 8), np.float32)
        oh[0, c] = 1.0
        m["onehot"] = oh
        in_maps.append(m)
    ctxA = _get_ctx("A", build_launch_A)
    resA = run_bass_kernel_spmd(ctxA.nc, in_maps, core_ids=list(range(8)))
    x1 = [resA.results[c]["x1_out"] for c in range(8)]
    qA = [resA.results[c]["q_out"] for c in range(8)]
    kA = [resA.results[c]["k_out"] for c in range(8)]
    vA = [resA.results[c]["v_out"] for c in range(8)]

    sharedB = dict(
        woT=np.ascontiguousarray(f32("L0_attn_Wo").T),
        gamma_in=f32("L0_attn_gamma")[None, :],
        lq1_in=f32("L0_attn_lambda_q1")[None, :],
        lk1_in=f32("L0_attn_lambda_k1")[None, :],
        lq2_in=f32("L0_attn_lambda_q2")[None, :],
        lk2_in=f32("L0_attn_lambda_k2")[None, :],
        gate_wT=sharedA["gate_wT"],
        ident_in=ident,
    )
    in_maps = []
    for c in range(8):
        b = c // 4
        m = dict(sharedB)
        m["x1_in"] = np.ascontiguousarray(x1[c])
        m["q_in"] = np.ascontiguousarray(qA[c])
        m["k_in"] = np.ascontiguousarray(np.concatenate(kA[4 * b:4 * b + 4], axis=1))
        m["vt_in"] = np.ascontiguousarray(np.concatenate(vA[4 * b:4 * b + 4], axis=1).T)
        in_maps.append(m)
    ctxB = _get_ctx("B", build_launch_B)
    resB = run_bass_kernel_spmd(ctxB.nc, in_maps, core_ids=list(range(8)))
    x2 = [resB.results[c]["x2_out"] for c in range(8)]
    x2b = [resB.results[c]["x2b_out"] for c in range(8)]
    cnts = np.concatenate([resB.results[c]["cnt_out"] for c in range(8)], axis=0)

    sharedC = dict(
        cnts_in=np.ascontiguousarray(cnts.astype(np.float32)),
        gate_wT=sharedA["gate_wT"],
        w1T=sharedA["w1T"], w3T=sharedA["w3T"], w2T=sharedA["w2T"],
        wguT=np.ascontiguousarray(f32("L1_ffn_w_gate_up").T),
        wdT=np.ascontiguousarray(f32("L1_ffn_w_down").T),
        wqT=np.ascontiguousarray(f32("L1_attn_wq")[permq1].T),
        wkT=np.ascontiguousarray(f32("L1_attn_wk")[permk1].T),
        wvT=np.ascontiguousarray(f32("L1_attn_wv").T),
        gq_r=f32("L1_attn_gq")[None, :],
        gk_r=f32("L1_attn_gk")[None, :],
        gv_r=f32("L1_attn_gv")[None, :],
        ident_in=ident,
    )
    in_maps = []
    for c in range(8):
        m = dict(sharedC)
        m["x2_in"] = np.ascontiguousarray(x2[c])
        m["x2b_in"] = np.ascontiguousarray(x2b[c])
        m["pos"] = np.arange((c % 4) * 1024, (c % 4) * 1024 + 1024, dtype=np.float32)[None, :]
        sp = np.zeros((8, 1), np.float32)
        sp[:c, 0] = 1.0
        m["selp_in"] = sp
        in_maps.append(m)
    ctxC = _get_ctx("C", build_launch_C)
    resC = run_bass_kernel_spmd(ctxC.nc, in_maps, core_ids=list(range(8)))
    x4 = [resC.results[c]["x4_out"] for c in range(8)]
    qC = [resC.results[c]["q_out"] for c in range(8)]
    kC = [resC.results[c]["k_out"] for c in range(8)]
    vC = [resC.results[c]["v_out"] for c in range(8)]

    sharedD = dict(
        woT=np.ascontiguousarray(f32("L1_attn_wo").T),
        go_r=f32("L1_attn_go")[None, :],
        wguT=sharedC["wguT"], wdT=sharedC["wdT"],
        ident_in=ident,
    )
    in_maps = []
    for c in range(8):
        b, j = c // 4, c % 4
        rot = 1024 * j
        kfull = np.concatenate(kC[4 * b:4 * b + 4], axis=1)
        vtfull = np.concatenate(vC[4 * b:4 * b + 4], axis=1).T
        kposr = np.roll(np.arange(4096, dtype=np.float32), -rot)
        m = dict(sharedD)
        m["x4_in"] = np.ascontiguousarray(x4[c])
        m["q_in"] = np.ascontiguousarray(qC[c])
        m["k_in"] = np.ascontiguousarray(np.roll(kfull, -rot, axis=1))
        m["vt_in"] = np.ascontiguousarray(np.roll(vtfull, -rot, axis=0))
        m["kposr_in"] = np.ascontiguousarray(kposr[None, :1024])
        m["ktminr_in"] = np.ascontiguousarray(kposr[None, ::128])
        m["kqr_in"] = np.ascontiguousarray(kposr[None, ::1024])
        m["pos"] = np.arange(j * 1024, j * 1024 + 1024, dtype=np.float32)[None, :]
        in_maps.append(m)
    ctxD = _get_ctx("D", build_launch_D)
    resD = run_bass_kernel_spmd(ctxD.nc, in_maps, core_ids=list(range(8)))

    out = np.zeros((2, 4096, 2048), np.float32)
    for c in range(8):
        b, j = c // 4, c % 4
        out[b, j * 1024:(j + 1) * 1024, :] = resD.results[c]["x6_out"].T
    return out
